# version 02
# baseline (speedup 1.0000x reference)
import numpy as np
import concourse.bass as bass
import concourse.mybir as mybir
from concourse.bass_utils import run_bass_kernel_spmd

F32 = mybir.dt.float32
BF16 = mybir.dt.bfloat16
ALU = mybir.AluOpType
AF = mybir.ActivationFunctionType
AX = mybir.AxisListType

ENGINES = ("sync", "scalar", "vector", "tensor", "gpsimd")
NT = 4096
TT = 512
NCHIP = 8


class Prog:

    def __init__(self, nc):
        self.nc = nc
        self.sched = {e: [] for e in ENGINES}
        self.ctrs = {}
        self.gseq = 0

    def _ctr(self, name, kind, eng):
        if name not in self.ctrs:
            self.ctrs[name] = [self.nc.alloc_semaphore(name), 0, kind, eng]
        return self.ctrs[name]

    def ev(self, eng, fn, waits=(), ctr=None, flush=None):
        if flush is None:
            flush = eng in ("vector", "gpsimd", "scalar")
        name = ctr or ("ev_" + eng)
        c = self._ctr(name, "ev", eng)
        c[1] += 1
        self.sched[eng].append((tuple(w for w in waits if w is not None), fn,
                                (name, 1, flush, c[1]), self.gseq))
        self.gseq += 1
        return (name, c[1])

    def raw(self, eng, fn, waits=()):
        self.sched[eng].append((tuple(w for w in waits if w is not None), fn, None,
                                self.gseq))
        self.gseq += 1
        return None

    def dma(self, q, out, in_, waits=(), ctr=None, accum=False):
        name = ctr or ("dma_" + q)
        c = self._ctr(name, "dma", q)
        c[1] += 16
        if accum:
            fn = lambda e, o=out, i=in_: e.dma_start(out=o, in_=i, accum_op=ALU.add)
        else:
            fn = lambda e, o=out, i=in_: e.dma_start(out=o, in_=i)
        self.sched[q].append((tuple(w for w in waits if w is not None), fn,
                              (name, 16, False, c[1]), self.gseq))
        self.gseq += 1
        return (name, c[1])

    def _prune(self):
        entries = []
        for e in ENGINES:
            for (w, fn, inc, g) in self.sched[e]:
                entries.append((g, e, w, inc))
        entries.sort()
        vc_eng = {e: {} for e in ENGINES}
        vc_inc = {}
        kept = {}
        for g, e, waits, inc in entries:
            vce = vc_eng[e]
            kw = []
            waits = sorted(waits, key=lambda t: -len(vc_inc.get(t, ())))
            for (cn, v) in waits:
                c = self.ctrs[cn]
                if c[2] == "ev" and c[3] == e:
                    continue
                if vce.get(cn, 0) >= v:
                    continue
                kw.append((cn, v))
                clk = vc_inc.get((cn, v))
                if clk:
                    for k2, v2 in clk.items():
                        if v2 > vce.get(k2, 0):
                            vce[k2] = v2
                vce[cn] = max(vce.get(cn, 0), v)
            kept[g] = kw
            if inc is not None:
                cn, k, fl, val = inc
                clk = dict(vce)
                clk[cn] = val
                vc_inc[(cn, val)] = clk
                if fl:
                    vce[cn] = max(vce.get(cn, 0), val)
        return kept

    def emit(self):
        nc = self.nc
        kept = self._prune()
        with nc.Block() as block:
            for e in ENGINES:
                if not self.sched[e]:
                    continue
                def make(e):
                    def body(eng):
                        for waits, fn, inc, g in self.sched[e]:
                            for (cn, v) in kept[g]:
                                eng.wait_ge(self.ctrs[cn][0], v)
                            inst = fn(eng)
                            if inc is not None:
                                cn, k, fl, val = inc
                                inst.then_inc(self.ctrs[cn][0], k)
                                if fl:
                                    eng.wait_ge(self.ctrs[cn][0], val)
                    return body
                getattr(block, e)(make(e))


def last(*toks):
    return [t for t in toks if t is not None]



class Consts:

    def __init__(self, nc, P):
        self.nc = nc
        ident = np.eye(128, dtype=np.float32)
        ones_row = np.ones((1, 128), dtype=np.float32)
        ones_col = np.full((128, 1), 1.0/1024.0, np.float32)
        ones_col1 = np.ones((128, 1), np.float32)
        jw6 = np.broadcast_to(np.arange(6, dtype=np.float32), (128, 6)).copy()
        slt = np.tril(np.ones((128, 128), np.float32), -1)
        sut = np.triu(np.ones((128, 128), np.float32), 1)
        plc = np.zeros((32, 384), np.float32)
        for sb in range(3):
            for p in range(32):
                plc[p, sb*128 + 32*sb + p] = 1.0
        self.ident = nc.alloc_sbuf_tensor("c_ident", [128, 128], F32)
        self.ones_row = nc.alloc_sbuf_tensor("c_onesrow", [1, 128], F32)
        self.mean_w = nc.alloc_sbuf_tensor("c_meanw", [128, 1], F32)
        self.ones_col = nc.alloc_sbuf_tensor("c_onescol", [128, 1], F32)
        self.jw6 = nc.alloc_sbuf_tensor("c_jw6", [128, 6], F32)
        self.slt = nc.alloc_sbuf_tensor("c_slt", [128, 128], F32)
        self.sut = nc.alloc_sbuf_tensor("c_sut", [128, 128], F32)
        self.eye32 = nc.alloc_sbuf_tensor("c_eye32", [32, 32], F32)
        self.plc = nc.alloc_sbuf_tensor("c_plc", [32, 384], F32)
        toks = []
        for name, dat, tile in [
            ("ident", ident, self.ident), ("onesrow", ones_row, self.ones_row),
            ("meanw", ones_col, self.mean_w), ("onescol", ones_col1, self.ones_col),
            ("jw6", jw6, self.jw6), ("slt", slt, self.slt), ("sut", sut, self.sut),
            ("eye32", np.eye(32, dtype=np.float32), self.eye32),
            ("plc", plc, self.plc),
        ]:
            d = nc.inline_tensor(np.ascontiguousarray(dat), "ct_" + name)
            toks.append(P.dma("sync", tile[:, :], d[:, :]))
        self.ready = toks[-1]



def emit_ln(P, S, src_dram, t0, T, out_bf, out_f32=None, dep=None):
    nc = P.nc
    xin = S["ln_x"]
    sq = S["scr_f0"]
    stat = S["ln_stat"]
    psum_s = S["pb"][6]
    psum_b = S["pb"][7]
    ldt = []
    for c in range(8):
        ldt.append(P.dma("sync", xin[c][:, :T], src_dram[c*128:(c+1)*128, t0:t0+T],
                         waits=last(dep, S.get("ln_war"))))
    def mm_sum(c):
        def f(t, c=c):
            return t.matmul(psum_s[0:1, :T], S["C"].mean_w[:, :], xin[c][:, :T],
                            start=(c == 0), stop=(c == 7))
        return f
    tmm = None
    for c in range(8):
        tmm = P.ev("tensor", mm_sum(c), waits=last(ldt[c], S["C"].ready if c == 0 else None))
    t_mu = P.ev("vector", lambda e: e.tensor_copy(stat[0:1, 0:T], psum_s[0:1, :T]),
                waits=last(tmm), flush=True)
    tsq = []
    for c in range(8):
        ts = P.ev("scalar", lambda e, c=c: e.activation(sq[:, :T], xin[c][:, :T], AF.Square),
                  waits=last(ldt[c], t_mu if c == 0 else None,
                             tsq[c-1] if c > 0 else None))
        def f(t, c=c):
            return t.matmul(psum_s[0:1, :T], S["C"].mean_w[:, :], sq[:, :T],
                            start=(c == 0), stop=(c == 7))
        tsq.append(P.ev("tensor", f, waits=last(ts)))
    t_ms = P.ev("vector", lambda e: e.tensor_copy(stat[0:1, TTK:TTK+T], psum_s[0:1, :T]),
                waits=last(tsq[-1]), flush=True)
    t_v0 = P.ev("vector", lambda e: e.tensor_mul(stat[0:1, 2*TTK:2*TTK+T], stat[0:1, 0:T],
                stat[0:1, 0:T]), waits=last(t_ms))
    t_v1 = P.ev("vector", lambda e: e.tensor_sub(stat[0:1, 2*TTK:2*TTK+T], stat[0:1, TTK:TTK+T],
                stat[0:1, 2*TTK:2*TTK+T]), waits=last(t_v0))
    t_v = P.ev("vector", lambda e: e.tensor_scalar(stat[0:1, 2*TTK:2*TTK+T],
               stat[0:1, 2*TTK:2*TTK+T], 1e-5, None, ALU.add), waits=last(t_v1))
    t_l = P.ev("scalar", lambda e: e.activation(stat[0:1, 2*TTK:2*TTK+T], stat[0:1, 2*TTK:2*TTK+T], AF.Ln),
               waits=last(t_v))
    t_a = P.ev("scalar", lambda e: e.activation(stat[0:1, 2*TTK:2*TTK+T], stat[0:1, 2*TTK:2*TTK+T], AF.Exp,
                                                scale=-0.5), waits=last(t_l))
    t_b = P.ev("vector", lambda e: e.scalar_tensor_tensor(stat[0:1, 3*TTK:3*TTK+T], stat[0:1, 0:T],
               -1.0, stat[0:1, 2*TTK:2*TTK+T], ALU.mult, ALU.mult), waits=last(t_a))
    t_ba = P.ev("tensor", lambda e: e.matmul(psum_b[:, :T], S["C"].ones_row[0:1, :],
                stat[0:1, 2*TTK:2*TTK+T], start=True, stop=True), waits=last(t_a, t_b))
    ab = S["ln_ab"]
    bb = S["ln_bb"]
    t_ea = P.ev("vector", lambda e: e.tensor_copy(ab[:, :T], psum_b[:, :T]), waits=last(t_ba), flush=True)
    t_bb = P.ev("tensor", lambda e: e.matmul(psum_b[:, :T], S["C"].ones_row[0:1, :],
                stat[0:1, 3*TTK:3*TTK+T], start=True, stop=True), waits=last(t_ea))
    t_eb = P.ev("vector", lambda e: e.tensor_copy(bb[:, :T], psum_b[:, :T]), waits=last(t_bb), flush=True)
    toks = []
    for c in range(8):
        t1 = P.ev("vector", lambda e, c=c: e.tensor_mul(sq[:, :T], xin[c][:, :T], ab[:, :T]),
                  waits=last(t_ea))
        if out_f32 is not None:
            t2 = P.ev("vector", lambda e, c=c: e.tensor_add(out_f32[c][:, :T], sq[:, :T],
                      bb[:, :T]), waits=last(t_eb, t1))
            t3 = P.ev("gpsimd", lambda e, c=c: e.tensor_copy(out_bf[c][:, :T],
                      out_f32[c][:, :T]), waits=last(t2))
            toks.append(t3)
        else:
            t2 = P.ev("vector", lambda e, c=c: e.tensor_add(sq[:, :T], sq[:, :T], bb[:, :T]),
                      waits=last(t_eb, t1))
            t3 = P.ev("vector", lambda e, c=c: e.tensor_copy(out_bf[c][:, :T], sq[:, :T]),
                      waits=last(t2))
            toks.append(t3)
    return toks, ldt



TTK = 256
NTILE = NT // TTK

def alloc_buffers(nc):
    S = {}
    S["A"] = [nc.alloc_sbuf_tensor(f"af{i}", [128, 512], F32) for i in range(14)]
    S["B"] = [nc.alloc_sbuf_tensor(f"bb{i}", [128, 512], BF16) for i in range(22)]
    S["WB"] = nc.alloc_sbuf_tensor("wbig", [128, 65536], BF16)
    S["pb"] = [nc.alloc_psum_tensor(f"pb{i}", [128, 512], F32) for i in range(8)]
    S["ln_stat"] = nc.alloc_sbuf_tensor("ln_stat", [1, 4*TTK], F32)
    S["rt_lg"] = nc.alloc_sbuf_tensor("rt_lg", [8, TTK], F32)
    S["rt_l2"] = nc.alloc_sbuf_tensor("rt_l2", [6, TTK], F32)
    S["rt_ah"] = nc.alloc_sbuf_tensor("rt_ah", [64, TTK], F32)
    S["rt_ahb"] = nc.alloc_sbuf_tensor("rt_ahb", [64, TTK+64], BF16)
    S["rt_tm"] = nc.alloc_sbuf_tensor("rt_tm", [128, 40], F32)
    S["rt_row"] = nc.alloc_sbuf_tensor("rt_row", [1, TTK], F32)
    S["w1"] = nc.alloc_sbuf_tensor("wst1", [128, 2048], BF16)
    S["w2"] = nc.alloc_sbuf_tensor("wst2", [128, 2048], BF16)
    S["pcol"] = nc.alloc_sbuf_tensor("pcol", [128, 40], F32)
    S["pcol2"] = nc.alloc_sbuf_tensor("pcol2", [128, 64], F32)
    S["carry"] = nc.alloc_sbuf_tensor("carry", [128, 40], F32)
    S["hcol"] = nc.alloc_sbuf_tensor("hcol", [128, 20], F32)
    S["misc"] = nc.alloc_sbuf_tensor("miscsb", [128, 256], F32)
    S["row1"] = nc.alloc_sbuf_tensor("row1", [1, 64], F32)
    return S


def view(tile, half, T=TTK):
    return tile[:, half*TTK:half*TTK + T]



def emit_routing(P, S, io, kind, t, xh_bf, dep):
    nc = P.nc
    C = S["C"]
    T = TTK
    t0 = t*TTK
    lg, l2, tm = S["rt_lg"], S["rt_l2"], S["rt_tm"]
    pb = S["pb"]
    rw = io["l0f_router"] if kind == "l0" else io["l1f_router"]
    wrt = S["w2"][:, 1984:2048]
    tw = P.dma("gpsimd", wrt.rearrange("p (a c) -> p a c", c=8), rw[:, :].rearrange("(a p) c -> p a c", p=128),
               waits=last(S.get("rt_war")))
    tmm = None
    for c in range(8):
        def f(e, c=c):
            return e.matmul(pb[5][0:8, :T], wrt[:, c*8:(c+1)*8], xh_bf[c][:, :T],
                            start=(c == 0), stop=(c == 7))
        tmm = P.ev("tensor", f, waits=last(tw, dep[c] if dep else None))
    t_lg = P.ev("scalar", lambda e: e.copy(lg[:, :T], pb[5][0:8, :T]), waits=last(tmm))
    t_l2 = None
    if kind == "l1":
        ta1 = P.dma("gpsimd", S["w2"][:, 0:400].rearrange("p (a c) -> p a c", c=50),
                    io["l1f_a1"][:, :].rearrange("(a p) c -> p a c", p=128), waits=last(S.get("rt_war")))
        for c in range(8):
            def f(e, c=c):
                return e.matmul(pb[4][0:50, :T], S["w2"][:, c*50:(c+1)*50], xh_bf[c][:, :T],
                                start=(c == 0), stop=(c == 7))
            tmm2 = P.ev("tensor", f, waits=last(ta1, dep[c] if dep else None))
        ah = S["rt_ah"]
        tb1 = P.dma("sync", S["pcol2"][0:50, 0:1], io["l1f_b1"][:, :], waits=last(S.get("rt_war")))
        th = P.ev("vector", lambda e: e.tensor_scalar(ah[0:50, :T], pb[4][0:50, :T],
                  S["pcol2"][0:50, 0:1], None, ALU.add), waits=last(tmm2, tb1), flush=True)
        t2 = P.ev("scalar", lambda e: e.activation(S["misc"][0:50, :T], ah[0:50, :T], AF.Square),
                  waits=last(th))
        t3 = P.ev("vector", lambda e: e.tensor_scalar(S["misc"][0:50, :T], S["misc"][0:50, :T],
                  0.044715, 1.0, ALU.mult, ALU.add), waits=last(t2))
        t4 = P.ev("vector", lambda e: e.scalar_tensor_tensor(S["misc"][0:50, :T], S["misc"][0:50, :T],
                  0.7978845608028654, ah[0:50, :T], ALU.mult, ALU.mult), waits=last(t3))
        t5 = P.ev("scalar", lambda e: e.activation(S["misc"][0:50, :T], S["misc"][0:50, :T], AF.Tanh),
                  waits=last(t4))
        t6 = P.ev("vector", lambda e: e.scalar_tensor_tensor(ah[0:50, :T], S["misc"][0:50, :T],
                  1.0, ah[0:50, :T], ALU.add, ALU.mult), waits=last(t5))
        t7 = P.ev("vector", lambda e: e.tensor_copy(S["rt_ahb"][0:50, :T], ah[0:50, :T]),
                  waits=last(t6))
        ta2 = P.dma("gpsimd", S["pcol2"][0:50, 8:14], io["l1f_a2"][:, :], waits=last(S.get("rt_war")))
        ta2b = P.ev("vector", lambda e: e.tensor_scalar(S["pcol2"][0:50, 16:22],
                    S["pcol2"][0:50, 8:14], 0.5, None, ALU.mult), waits=last(ta2))
        ta2c = P.ev("vector", lambda e: e.tensor_copy(S["rt_ahb"][0:50, 256:262],
                    S["pcol2"][0:50, 16:22]), waits=last(ta2b))
        tl2 = P.ev("tensor", lambda e: e.matmul(pb[4][0:6, :T], S["rt_ahb"][0:50, 256:262],
                   S["rt_ahb"][0:50, :T], start=True, stop=True), waits=last(t7, ta2c))
        tb2 = P.dma("sync", S["pcol2"][0:6, 24:25], io["l1f_b2"][:, :], waits=last(S.get("rt_war")))
        tc1 = P.ev("vector", lambda e: e.tensor_scalar(l2[:, :T], pb[4][0:6, :T],
                   S["pcol2"][0:6, 24:25], -30.0, ALU.add, ALU.max), waits=last(tl2, tb2), flush=True)
        t_l2 = P.ev("vector", lambda e: e.tensor_single_scalar(l2[:, :T], l2[:, :T],
                    30.0, ALU.min), waits=last(tc1))
    outdram = io["gates0"] if kind == "l0" else io["scores1"]
    tprev = None
    for s in range(T // 128):
        sl = slice(s*128, (s+1)*128)
        tt = P.ev("tensor", lambda e, sl=sl: e.transpose(pb[6][:, 0:8], lg[:, sl], C.ident[0:8, 0:8]),
                  waits=last(t_lg, tprev))
        te = P.ev("vector", lambda e: e.tensor_copy(tm[:, 0:8], pb[6][:, 0:8]), waits=last(tt), flush=True)
        t1 = P.ev("vector", lambda e: e.reduce_max(tm[:, 8:9], tm[:, 0:8], AX.X), waits=last(te), flush=True)
        t2 = P.ev("vector", lambda e: e.tensor_scalar(tm[:, 0:8], tm[:, 0:8], tm[:, 8:9],
                  None, ALU.subtract), waits=last(t1))
        t3 = P.ev("scalar", lambda e: e.activation(tm[:, 0:8], tm[:, 0:8], AF.Exp), waits=last(t2))
        t4 = P.ev("vector", lambda e: e.reduce_sum(tm[:, 8:9], tm[:, 0:8], AX.X), waits=last(t3))
        t5 = P.ev("vector", lambda e: e.reciprocal(tm[:, 9:10], tm[:, 8:9]), waits=last(t4), flush=True)
        t6 = P.ev("vector", lambda e: e.tensor_scalar(tm[:, 0:8], tm[:, 0:8], tm[:, 9:10],
                  None, ALU.mult), waits=last(t5), flush=True)
        tr = t6
        for e2 in range(8):
            ta = P.ev("vector", lambda e, e2=e2: e.tensor_single_scalar(tm[:, 16:24], tm[:, 0:8],
                      tm[:, e2:e2+1], ALU.is_gt), waits=last(tr))
            tr = P.ev("vector", lambda e, e2=e2: e.reduce_sum(tm[:, 24+e2:25+e2], tm[:, 16:24],
                      AX.X), waits=last(ta))
        if kind == "l0":
            tk = P.ev("vector", lambda e: e.tensor_single_scalar(tm[:, 16:24], tm[:, 24:32],
                      2.0, ALU.is_lt), waits=last(tr))
        else:
            ttk = P.ev("tensor", lambda e, sl=sl: e.transpose(pb[6][:, 8:14], l2[:, sl],
                       C.ident[0:6, 0:6]), waits=last(t_l2, te))
            tke = P.ev("vector", lambda e: e.tensor_copy(tm[:, 32:38], pb[6][:, 8:14]), waits=last(ttk), flush=True)
            tm1 = P.ev("vector", lambda e: e.reduce_max(tm[:, 38:39], tm[:, 32:38], AX.X), waits=last(tke), flush=True)
            tm2 = P.ev("vector", lambda e: e.tensor_scalar(tm[:, 32:38], tm[:, 32:38], tm[:, 38:39],
                       None, ALU.is_ge), waits=last(tm1))
            tm3 = P.ev("vector", lambda e: e.tensor_mul(tm[:, 32:38], tm[:, 32:38], C.jw6[:, :]),
                       waits=last(tm2))
            tm4 = P.ev("vector", lambda e: e.reduce_sum(tm[:, 38:39], tm[:, 32:38], AX.X), waits=last(tm3))
            tm5 = P.ev("vector", lambda e: e.tensor_single_scalar(tm[:, 38:39], tm[:, 38:39],
                       1.0, ALU.add), waits=last(tm4), flush=True)
            tk = P.ev("vector", lambda e: e.tensor_scalar(tm[:, 16:24], tm[:, 24:32], tm[:, 38:39],
                      None, ALU.is_lt), waits=last(tm5, tr))
        tsc = P.ev("vector", lambda e: e.tensor_mul(tm[:, 0:8], tm[:, 0:8], tm[:, 16:24]),
                   waits=last(tk))
        tt2 = P.ev("tensor", lambda e, sl=sl: e.transpose(pb[6][0:8, 16:144], tm[:, 0:8],
                   C.ident[:, :]), waits=last(tsc))
        te2 = P.ev("vector", lambda e, sl=sl: e.tensor_copy(lg[:, sl], pb[6][0:8, 16:144]),
                   waits=last(tt2), flush=True)
        tprev = P.dma("sync", outdram[0:8, t0+s*128:t0+(s+1)*128], lg[:, sl], waits=last(te2))
    S["rt_war"] = tprev
    return tprev



def emit_l0f(P, S, io, st_in, st_out, phase_dep):
    nc = P.nc
    C = S["C"]
    A, B, WB, pb = S["A"], S["B"], S["WB"], S["pb"]
    T = TTK
    S["ln_x"] = [view(A[c//2], c % 2) for c in range(8)]
    S["scr_f0"] = view(A[8], 0)
    S["ln_ab"] = view(A[9], 0)
    S["ln_bb"] = view(A[9], 1)
    xh_bf = [view(B[c//2], c % 2) for c in range(8)]
    xh_f32 = [view(A[4 + c//2], c % 2) for c in range(8)]
    S["ln_war"] = phase_dep
    dep = phase_dep
    stoks = None
    for t in range(NTILE):
        toks, _ = emit_ln(P, S, st_in, t*T, T, xh_bf, xh_f32, dep=dep)
        stoks = []
        for c in range(8):
            stoks.append(P.dma("scalar", io["xhbf"][c*128:(c+1)*128, t*T:(t+1)*T],
                               xh_bf[c][:, :T], waits=last(toks[c])))
            stoks.append(P.dma("scalar", io["xhf32"][c*128:(c+1)*128, t*T:(t+1)*T],
                               xh_f32[c][:, :T], waits=last(toks[c])))
        trt = emit_routing(P, S, io, "l0", t, xh_bf, toks)
        S["ln_war"] = stoks[-1]
        dep = trt
    phaseA_end = trt
    xh_war = stoks[-1]
    NEXP = 9
    xt = [view(B[c//2], c % 2) for c in range(8)]
    gated = [B[4 + fb//2][:, (fb % 2)*T:(fb % 2)*T + T] for fb in range(32)]
    gel = view(A[10], 0)
    gb_bf = view(B[20], 0)
    scr = [view(A[11], 0), view(A[11], 1)]
    xhf = [view(A[c//2], c % 2) for c in range(8)]
    xold = [view(A[4 + c//2], c % 2) for c in range(8)]
    t_up = {}; t_gate = {}; t_evac = {}; t_store = {}; t_gbread = {}
    t_wU = t_wD = None
    t_xt = phaseA_end
    t_gb = None
    for e in range(NEXP):
        u_dram = io["l0f_up"][e]
        d_dram = io["l0f_dn"][e]
        wwaits = last(t_up.get((e-1, NTILE-1, 31)), t_wU, phase_dep if e == 0 else None)
        for h2 in range(2):
            t_wU = P.dma("gpsimd", WB[:, h2*16384:(h2+1)*16384].rearrange("p (a f) -> p a f", f=4096),
                         u_dram.rearrange("(a p) f -> p a f", p=128)[:, h2*4:(h2+1)*4, :],
                         waits=wwaits if h2 == 0 else ())
        dwaits = last(t_evac.get((e-1, NTILE-1, 7)), t_evac.get((e-1, NTILE-1, 6)), t_wD,
                      phase_dep if e == 0 else None)
        for h2 in range(2):
            t_wD = P.dma("gpsimd", WB[:, 32768 + h2*16384:32768 + (h2+1)*16384].rearrange("p (a f) -> p a f", f=1024),
                         d_dram.rearrange("(a p) f -> p a f", p=128)[:, h2*16:(h2+1)*16, :],
                         waits=dwaits if h2 == 0 else ())
        for t in range(NTILE):
            ldx = []
            for c in range(8):
                ldx.append(P.dma("sync", xt[c][:, :T], io["xhbf"][c*128:(c+1)*128, t*T:(t+1)*T],
                                 waits=last(t_xt, xh_war if (e == 0 and t == 0) else None)))
            if e > 0:
                trow = P.dma("sync", S["rt_row"][0:1, :T], io["gates0"][e-1:e, t*T:(t+1)*T],
                             waits=last(t_gbread.get("row")))
                tbc = P.ev("tensor", lambda e2: e2.matmul(pb[4][:, :T], C.ones_row[0:1, :],
                           S["rt_row"][0:1, :T], start=True, stop=True),
                           waits=last(trow, t_gb))
                t_gbread["row"] = tbc
                t_gb = P.ev("vector", lambda e2: e2.tensor_copy(gb_bf[:, :T], pb[4][:, :T]),
                            waits=last(tbc, t_gate.get(("last_gb",))), flush=True)
            for fb in range(32):
                bank = pb[fb % 2]
                def up(e2, fb=fb, bank=bank):
                    for kc in range(8):
                        i = e2.matmul(bank[:, :T], WB[:, kc*4096 + fb*128:kc*4096 + (fb+1)*128],
                                      xt[kc][:, :T], start=(kc == 0), stop=(kc == 7))
                    return i
                tu = P.ev("tensor", up,
                          waits=last(t_wU, ldx[7],
                                     t_gate.get((e, t, fb-2)),
                                     t_gate.get((e, t-1, 30 + fb % 2)) if (fb < 2 and t > 0) else None,
                                     t_gate.get((e-1, NTILE-1, 30 + fb % 2)) if (fb < 2 and t == 0) else None))
                t_up[(e, t, fb)] = tu
                if e == 0:
                    tg = P.ev("scalar", lambda e2, fb=fb, bank=bank: e2.activation(
                        gated[fb][:, :], bank[:, :T], AF.Gelu), waits=last(tu))
                else:
                    tg1 = P.ev("scalar", lambda e2, bank=bank: e2.activation(
                        gel[:, :T], bank[:, :T], AF.Gelu),
                        waits=last(tu, t_gate.get((e, t, fb-1)),
                                   t_gate.get((e, t-1, 31)) if (fb == 0 and t > 0) else None,
                                   t_gate.get((e-1, NTILE-1, 31)) if (fb == 0 and t == 0) else None))
                    tg = P.ev("gpsimd", lambda e2, fb=fb: e2.tensor_mul(
                        gated[fb][:, :], gel[:, :T], gb_bf[:, :T]), waits=last(tg1, t_gb))
                t_gate[(e, t, fb)] = tg
                if fb == 31:
                    t_gate[("last_gb",)] = tg
            tglast = t_gate[(e, t, 31)]
            if e == 0:
                ldf = [P.dma("sync", xhf[c][:, :T], io["xhf32"][c*128:(c+1)*128, t*T:(t+1)*T],
                             waits=last(xh_war if t == 0 else None,
                                        t_evac.get((0, t-1, 7)), t_evac.get((0, t-1, 6))))
                       for c in range(8)]
                ldo = [P.dma("sync", xold[c][:, :T], st_in[c*128:(c+1)*128, t*T:(t+1)*T],
                             waits=last(xh_war if t == 0 else None,
                                        t_evac.get((0, t-1, 7)), t_evac.get((0, t-1, 6))))
                       for c in range(8)]
            for oc in range(8):
                bank = pb[2 + oc % 2]
                def dn(e2, oc=oc, bank=bank):
                    for fb in range(32):
                        i = e2.matmul(bank[:, :T], WB[:, 32768 + fb*1024 + oc*128:32768 + fb*1024 + (oc+1)*128],
                                      gated[fb][:, :], start=(fb == 0), stop=(fb == 31))
                    return i
                t_dn = P.ev("tensor", dn,
                            waits=last(t_wD, tglast,
                                       t_evac.get((e, t, oc-2)),
                                       t_evac.get((e, t-1, 6 + oc % 2)) if (oc < 2 and t > 0) else None,
                                       t_evac.get((e-1, NTILE-1, 6 + oc % 2)) if (oc < 2 and t == 0) else None))
                sc = scr[oc % 2]
                scwar = last(t_store.get((t, oc-2)),
                             t_store.get((t-1, 6 + oc % 2)) if (oc < 2 and t > 0) else None,
                             t_store.get((NTILE-1, 6 + oc % 2)) if (oc < 2 and t == 0) else None)
                if e == 0:
                    t1 = P.ev("vector", lambda e2, oc=oc, bank=bank, sc=sc: e2.tensor_add(
                        sc[:, :T], bank[:, :T], xhf[oc][:, :T]),
                        waits=last(t_dn, ldf[oc]) + scwar, flush=True)
                    tev = P.ev("vector", lambda e2, oc=oc, sc=sc: e2.tensor_add(
                        sc[:, :T], sc[:, :T], xold[oc][:, :T]), waits=last(ldo[oc]))
                else:
                    tev = P.ev("scalar", lambda e2, bank=bank, sc=sc: e2.copy(
                        sc[:, :T], bank[:, :T]), waits=last(t_dn) + scwar)
                t_evac[(e, t, oc)] = tev
                t_store[(t, oc)] = P.dma("gpsimd", st_out[oc*128:(oc+1)*128, t*T:(t+1)*T],
                                         sc[:, :T], waits=last(tev, t_store.get((t, oc))),
                                         accum=(e > 0))
            t_xt = t_up[(e, t, 31)]
    return t_store[(NTILE-1, 7)]



PHASES_ALL = ("l0a", "l0f", "l1a", "l1f")

def build_program(phases=PHASES_ALL, dbg=False):
    nc = bass.Bass("TRN2", target_bir_lowering=False)
    P = Prog(nc)
    io = {}
    DKIND = "ExternalOutput" if dbg else "Internal"
    active = [p for p in PHASES_ALL if p in phases]
    chain = {}
    first = active[0]
    if first == "l0a":
        io["x"] = nc.dram_tensor("x", [1024, NT + 3], F32, kind="ExternalInput")
    else:
        io["pin"] = nc.dram_tensor("pin", [1024, NT], F32, kind="ExternalInput")
    for i, ph in enumerate(active):
        if i + 1 < len(active):
            chain[ph] = nc.dram_tensor("st_" + ph, [1024, NT], F32, kind="Internal")
        else:
            chain[ph] = nc.dram_tensor("out", [1024, NT], F32, kind="ExternalOutput")
    if "l0f" in phases or "l1f" in phases:
        io["xhbf"] = nc.dram_tensor("xhbf", [1024, NT], BF16, kind=DKIND)
        io["xhf32"] = nc.dram_tensor("xhf32", [1024, NT], F32, kind=DKIND)
    if "l0f" in phases:
        io["gates0"] = nc.dram_tensor("gates0", [8, NT], F32, kind=DKIND)
        io["l0f_router"] = nc.dram_tensor("l0f_router", [1024, 8], F32, kind="ExternalInput")
        io["l0f_up"] = [nc.dram_tensor(f"l0f_up{e}", [1024, 4096], F32, kind="ExternalInput")
                        for e in range(9)]
        io["l0f_dn"] = [nc.dram_tensor(f"l0f_dn{e}", [4096, 1024], F32, kind="ExternalInput")
                        for e in range(9)]
    if "l1f" in phases:
        io["scores1"] = nc.dram_tensor("scores1", [8, NT], F32, kind="Internal")
        io["l1f_router"] = nc.dram_tensor("l1f_router", [1024, 8], F32, kind="ExternalInput")
        io["l1f_a1"] = nc.dram_tensor("l1f_a1", [1024, 50], F32, kind="ExternalInput")
        io["l1f_b1"] = nc.dram_tensor("l1f_b1", [50, 1], F32, kind="ExternalInput")
        io["l1f_a2"] = nc.dram_tensor("l1f_a2", [50, 6], F32, kind="ExternalInput")
        io["l1f_b2"] = nc.dram_tensor("l1f_b2", [6, 1], F32, kind="ExternalInput")
        io["l1f_gw"] = [nc.dram_tensor(f"l1f_gw{e}", [1024, 2688], F32, kind="ExternalInput")
                        for e in range(8)]
        io["l1f_uw"] = [nc.dram_tensor(f"l1f_uw{e}", [1024, 2688], F32, kind="ExternalInput")
                        for e in range(8)]
        io["l1f_dw"] = [nc.dram_tensor(f"l1f_dw{e}", [2688, 1024], F32, kind="ExternalInput")
                        for e in range(8)]
    if "l0a" in phases:
        if dbg:
            for n, p in [("dbg_pw", 32), ("dbg_den", 32), ("dbg_cs", 32), ("dbg_xv", 128),
                         ("dbg_nre", 128), ("dbg_gate", 128), ("dbg_sraw", 32), ("dbg_xh", 128),
                         ("dbg_phi", 128), ("dbg_sin", 128), ("dbg_cos", 128), ("dbg_rem", 128),
                         ("dbg_pwb", 128), ("dbg_nre2", 128), ("dbg_nim2", 128), ("dbg_gtd", 128), ("dbg_csb", 128), ("dbg_q", 32), ("dbg_s2", 32), ("dbg_tails", 128), ("dbg_xho", 128)]:
                io[n] = nc.dram_tensor(n, [p, NT], F32, kind="ExternalOutput")
        io["l0a_inproj"] = nc.dram_tensor("l0a_inproj", [1024, 4128], F32, kind="ExternalInput")
        io["l0a_convk"] = nc.dram_tensor("l0a_convk", [128, 132], F32, kind="ExternalInput")
        io["l0a_convb"] = nc.dram_tensor("l0a_convb", [128, 33], F32, kind="ExternalInput")
        io["l0a_theta"] = nc.dram_tensor("l0a_theta", [128, 16], F32, kind="ExternalInput")
        io["l0a_slopes"] = nc.dram_tensor("l0a_slopes", [32, 1], F32, kind="ExternalInput")
        io["l0a_sscale"] = nc.dram_tensor("l0a_sscale", [32, 1], F32, kind="ExternalInput")
        io["l0a_dlg"] = nc.dram_tensor("l0a_dlg", [1, 3], F32, kind="ExternalInput")
        io["l0a_wrebd"] = nc.dram_tensor("l0a_wrebd", [128, 128], F32, kind="ExternalInput")
        io["l0a_wimbd"] = nc.dram_tensor("l0a_wimbd", [128, 128], F32, kind="ExternalInput")
        io["l0a_nsre"] = nc.dram_tensor("l0a_nsre", [128, 16], F32, kind="ExternalInput")
        io["l0a_nsim"] = nc.dram_tensor("l0a_nsim", [128, 16], F32, kind="ExternalInput")
        io["l0a_outproj"] = nc.dram_tensor("l0a_outproj", [2048, 1024], F32, kind="ExternalInput")
        io["l0a_dist"] = nc.dram_tensor("l0a_dist", [32, NT], F32, kind="ExternalInput")
        io["cin"] = nc.dram_tensor("cin", [128, 33], F32, kind="ExternalInput")
        io["cout"] = nc.dram_tensor("cout", [128, 33], F32, kind="ExternalOutput")
    if "l1a" in phases:
        io["l1a_win"] = nc.dram_tensor("l1a_win", [1024, 4096], F32, kind="ExternalInput")
        io["l1a_wl"] = nc.dram_tensor("l1a_wl", [2048, 2048], F32, kind="ExternalInput")
        io["l1a_bl"] = nc.dram_tensor("l1a_bl", [2048, 1], F32, kind="ExternalInput")
        io["l1a_wv"] = nc.dram_tensor("l1a_wv", [2048, 2048], F32, kind="ExternalInput")
        io["l1a_bv"] = nc.dram_tensor("l1a_bv", [2048, 1], F32, kind="ExternalInput")
        io["l1a_wout"] = nc.dram_tensor("l1a_wout", [2048, 1024], F32, kind="ExternalInput")
        io["l1a_omega"] = nc.dram_tensor("l1a_omega", [128, 16], F32, kind="ExternalInput")
        io["l1a_lmask"] = nc.dram_tensor("l1a_lmask", [128, 1], F32, kind="ExternalInput")
        io["hin"] = nc.dram_tensor("hin", [128, 16], F32, kind="ExternalInput")
        io["hout"] = nc.dram_tensor("hout", [128, 16], F32, kind="ExternalOutput")
        io["xi_b"] = nc.dram_tensor("xi_b", [2048, NT], BF16, kind="Internal")
        io["pre_f"] = nc.dram_tensor("pre_f", [2048, NT], F32, kind="Internal")
        io["v_b"] = nc.dram_tensor("v_b", [2048, NT], BF16, kind="Internal")
        io["sz_b"] = nc.dram_tensor("sz_b", [2048, NT], BF16, kind="Internal")
    S = alloc_buffers(nc)
    S["C"] = Consts(nc, P)
    dep = None
    cur_in = io.get("x", io.get("pin"))
    for ph in active:
        st_out = chain[ph]
        if ph == "l0a":
            dep = emit_l0a(P, S, io, cur_in, st_out, dep)
        elif ph == "l0f":
            dep = emit_l0f(P, S, io, cur_in, st_out, dep)
        elif ph == "l1a":
            dep = emit_l1a(P, S, io, cur_in, st_out, dep)
        elif ph == "l1f":
            dep = emit_l1f(P, S, io, cur_in, st_out, dep)
        cur_in = st_out
    for q in ("gpsimd", "sync", "scalar"):
        cn = "dma_" + q
        if cn in P.ctrs:
            P.raw(q, lambda e: None, waits=[(cn, P.ctrs[cn][1])])
    P.emit()
    return nc, io



def _stage_shared(inp):
    w = {}
    if "L0_ffn_router_w" in inp:
        w["l0f_router"] = np.ascontiguousarray(inp["L0_ffn_router_w"], np.float32)
        ups = [inp["L0_ffn_shared_up"][0]] + [inp["L0_ffn_routed_up"][e] for e in range(8)]
        dns = [inp["L0_ffn_shared_down"][0]] + [inp["L0_ffn_routed_down"][e] for e in range(8)]
        for e in range(9):
            w[f"l0f_up{e}"] = np.ascontiguousarray(ups[e], np.float32)
            w[f"l0f_dn{e}"] = np.ascontiguousarray(dns[e], np.float32)
    if "L1_ffn_router_w" in inp:
        w["l1f_router"] = np.ascontiguousarray(inp["L1_ffn_router_w"].T, np.float32)
        w["l1f_a1"] = np.ascontiguousarray(inp["L1_ffn_actor_w1"].T, np.float32)
        w["l1f_b1"] = np.ascontiguousarray(inp["L1_ffn_actor_b1"].reshape(50, 1), np.float32)
        w["l1f_a2"] = np.ascontiguousarray(inp["L1_ffn_actor_w2"].T, np.float32)
        w["l1f_b2"] = np.ascontiguousarray(inp["L1_ffn_actor_b2"].reshape(6, 1), np.float32)
        for e in range(8):
            w[f"l1f_gw{e}"] = np.ascontiguousarray(inp["L1_ffn_gate_w"][e].T, np.float32)
            w[f"l1f_uw{e}"] = np.ascontiguousarray(inp["L1_ffn_up_w"][e].T, np.float32)
            w[f"l1f_dw{e}"] = np.ascontiguousarray(inp["L1_ffn_down_w"][e].T, np.float32)
    if "L0_attn_in_proj_w" in inp:
        w["l0a_inproj"] = np.ascontiguousarray(inp["L0_attn_in_proj_w"], np.float32)
        ck = inp["L0_attn_conv_k"].reshape(4, 4128).T.astype(np.float32)
        ckt = np.zeros((128, 132), np.float32)
        ckt[:, 0:128] = ck[0:4096].reshape(32, 128, 4).transpose(1, 0, 2).reshape(128, 128)
        ckt[0:32, 128:132] = ck[4096:4128]
        w["l0a_convk"] = ckt
        cb = inp["L0_attn_conv_b"].astype(np.float32)
        cbt = np.zeros((128, 33), np.float32)
        cbt[:, 0:32] = cb[0:4096].reshape(32, 128).T
        cbt[0:32, 32] = cb[4096:4128]
        w["l0a_convb"] = cbt
        w["l0a_theta"] = np.ascontiguousarray(inp["L0_attn_theta"].reshape(16, 128).T, np.float32)
        w["l0a_slopes"] = np.ascontiguousarray(
            np.concatenate([inp["L0_attn_decay_slopes"], inp["L0_attn_anchor_slopes"]]).reshape(32, 1), np.float32)
        w["l0a_sscale"] = np.ascontiguousarray(inp["L0_attn_score_scale"].reshape(32, 1), np.float32)
        w["l0a_dlg"] = np.ascontiguousarray(inp["L0_attn_deriv_logits"].reshape(1, 3), np.float32)
        bd = np.zeros((128, 128), np.float32)
        bd[0:64, 0:64] = inp["L0_attn_W_re"]; bd[64:128, 64:128] = inp["L0_attn_W_re"]
        w["l0a_wrebd"] = bd
        bd2 = np.zeros((128, 128), np.float32)
        bd2[0:64, 0:64] = inp["L0_attn_W_im"]; bd2[64:128, 64:128] = inp["L0_attn_W_im"]
        w["l0a_wimbd"] = bd2
        w["l0a_nsre"] = np.ascontiguousarray(np.tile(inp["L0_attn_norm_scale"][:64], 32).reshape(16, 128).T, np.float32)
        w["l0a_nsim"] = np.ascontiguousarray(np.tile(inp["L0_attn_norm_scale"][64:], 32).reshape(16, 128).T, np.float32)
        w["l0a_outproj"] = np.ascontiguousarray(inp["L0_attn_out_proj_w"], np.float32)
    if "L1_attn_Win" in inp:
        w["l1a_win"] = np.ascontiguousarray(inp["L1_attn_Win"].T, np.float32)
        w["l1a_wl"] = np.ascontiguousarray(inp["L1_attn_Wl"].T, np.float32)
        w["l1a_bl"] = np.ascontiguousarray(inp["L1_attn_bl"].reshape(2048, 1), np.float32)
        w["l1a_wv"] = np.ascontiguousarray(inp["L1_attn_Wv"].T, np.float32)
        w["l1a_bv"] = np.ascontiguousarray(inp["L1_attn_bv"].reshape(2048, 1), np.float32)
        w["l1a_wout"] = np.ascontiguousarray(inp["L1_attn_Wout"].T, np.float32)
        w["l1a_omega"] = np.ascontiguousarray(inp["L1_attn_omega"].reshape(16, 128).T, np.float32)
    return w


def _dist_matrix(half):
    pos = (half*NT + np.arange(NT)).astype(np.float32)
    D = np.empty((32, NT), np.float32)
    D[:28, :] = np.maximum(np.float32(8191.0) - pos, 0.0)[None, :]
    D[28:, :] = pos[None, :]
    return D



def emit_l1f(P, S, io, st_in, st_out, phase_dep):
    nc = P.nc
    C = S["C"]
    A, B, WB, pb = S["A"], S["B"], S["WB"], S["pb"]
    T = TTK
    S["ln_x"] = [view(A[c//2], c % 2) for c in range(8)]
    S["scr_f0"] = view(A[8], 0)
    S["ln_ab"] = view(A[9], 0)
    S["ln_bb"] = view(A[9], 1)
    xh_bf = [view(B[c//2], c % 2) for c in range(8)]
    S["ln_war"] = phase_dep
    dep = phase_dep
    stoks = None
    for t in range(NTILE):
        toks, _ = emit_ln(P, S, st_in, t*T, T, xh_bf, None, dep=dep)
        stoks = []
        for c in range(8):
            stoks.append(P.dma("scalar", io["xhbf"][c*128:(c+1)*128, t*T:(t+1)*T],
                               xh_bf[c][:, :T], waits=last(toks[c])))
        trt = emit_routing(P, S, io, "l1", t, xh_bf, toks)
        S["ln_war"] = stoks[-1]
        dep = trt
    phaseA_end = trt
    xh_war = stoks[-1]
    GB0, UB0, DB0 = 0, 21504, 43008
    NFB = 21
    xt = [view(B[c//2], c % 2) for c in range(8)]
    xs = [view(B[4 + c//2], c % 2) for c in range(8)]
    rt = [B[8 + fb//2][:, (fb % 2)*T:(fb % 2)*T + T] for fb in range(NFB)]
    gel = view(A[10], 0)
    sb_bf = view(B[20], 0)
    scr = [view(A[11], 0), view(A[11], 1)]
    xold = [view(A[4 + c//2], c % 2) for c in range(8)]
    t_r = {}; t_evac = {}; t_store = {}; t_up = {}
    t_wG = t_wU = t_wD = None
    t_xt = phaseA_end
    t_sb = None
    t_xs = {}
    for e in range(8):
        wwaits = last(t_up.get((e-1, NTILE-1, "g", NFB-1)), t_up.get((e-1, NTILE-1, "u", NFB-1)),
                      t_wG, phase_dep if e == 0 else None)
        for h2 in range(2):
            t_wG = P.dma("gpsimd", WB[:, GB0 + h2*10752:GB0 + (h2+1)*10752].rearrange("p (a f) -> p a f", f=2688),
                         io["l1f_gw"][e].rearrange("(a p) f -> p a f", p=128)[:, h2*4:(h2+1)*4, :],
                         waits=wwaits if h2 == 0 else ())
            t_wU = P.dma("gpsimd", WB[:, UB0 + h2*10752:UB0 + (h2+1)*10752].rearrange("p (a f) -> p a f", f=2688),
                         io["l1f_uw"][e].rearrange("(a p) f -> p a f", p=128)[:, h2*4:(h2+1)*4, :],
                         waits=())
        dwaits = last(t_evac.get((e-1, NTILE-1, 7)), t_evac.get((e-1, NTILE-1, 6)), t_wD,
                      phase_dep if e == 0 else None)
        for h2 in range(3):
            lo, hi = h2*7, min((h2+1)*7, NFB)
            t_wD = P.dma("gpsimd", WB[:, DB0 + lo*1024:DB0 + hi*1024].rearrange("p (a f) -> p a f", f=1024),
                         io["l1f_dw"][e].rearrange("(a p) f -> p a f", p=128)[:, lo:hi, :],
                         waits=dwaits if h2 == 0 else ())
        for t in range(NTILE):
            ldx = []
            for c in range(8):
                ldx.append(P.dma("sync", xt[c][:, :T], io["xhbf"][c*128:(c+1)*128, t*T:(t+1)*T],
                                 waits=last(t_xt, xh_war if (e == 0 and t == 0) else None)))
            trow = P.dma("sync", S["rt_row"][0:1, :T], io["scores1"][e:e+1, t*T:(t+1)*T],
                         waits=last(t_r.get("rowread")))
            tbc = P.ev("tensor", lambda e2: e2.matmul(pb[6][:, :T], C.ones_row[0:1, :],
                       S["rt_row"][0:1, :T], start=True, stop=True),
                       waits=last(trow, t_sb))
            t_r["rowread"] = tbc
            t_sb = P.ev("vector", lambda e2: e2.tensor_copy(sb_bf[:, :T], pb[6][:, :T]),
                        waits=last(tbc, t_xs.get((e, t)) if t > 0 else t_xs.get((e-1, NTILE))))
            txs = []
            for c in range(8):
                txs.append(P.ev("gpsimd", lambda e2, c=c: e2.tensor_mul(xs[c][:, :T],
                           xt[c][:, :T], sb_bf[:, :T]), waits=last(t_sb, ldx[c])))
            t_xs[(e, t+1)] = txs[-1]
            for fb in range(NFB):
                bg = pb[fb % 2]
                bu = pb[2 + fb % 2]
                def upg(e2, fb=fb, bg=bg):
                    for kc in range(8):
                        i = e2.matmul(bg[:, :T], WB[:, GB0 + kc*2688 + fb*128:GB0 + kc*2688 + (fb+1)*128],
                                      xt[kc][:, :T], start=(kc == 0), stop=(kc == 7))
                    return i
                def upu(e2, fb=fb, bu=bu):
                    for kc in range(8):
                        i = e2.matmul(bu[:, :T], WB[:, UB0 + kc*2688 + fb*128:UB0 + kc*2688 + (fb+1)*128],
                                      xs[kc][:, :T], start=(kc == 0), stop=(kc == 7))
                    return i
                war_g = last(t_r.get((e, t, fb-2)),
                             t_r.get((e, t-1, NFB-2 + fb % 2)) if (fb < 2 and t > 0) else None,
                             t_r.get((e-1, NTILE-1, NFB-2 + fb % 2)) if (fb < 2 and t == 0) else None)
                tg = P.ev("tensor", upg, waits=last(t_wG, ldx[7]) + war_g)
                t_up[(e, t, "g", fb)] = tg
                tu = P.ev("tensor", upu, waits=last(t_wU, txs[-1]) + war_g)
                t_up[(e, t, "u", fb)] = tu
                ts1 = P.ev("scalar", lambda e2, bg=bg: e2.activation(gel[:, :T], bg[:, :T],
                           AF.Silu),
                           waits=last(tg, t_r.get((e, t, fb-1)),
                                      t_r.get((e, t-1, NFB-1)) if (fb == 0 and t > 0) else None,
                                      t_r.get((e-1, NTILE-1, NFB-1)) if (fb == 0 and t == 0) else None))
                tr2 = P.ev("vector", lambda e2, fb=fb, bu=bu: e2.tensor_mul(rt[fb][:, :],
                           gel[:, :T], bu[:, :T]), waits=last(ts1, tu))
                t_r[(e, t, fb)] = tr2
            trlast = t_r[(e, t, NFB-1)]
            if e == 0:
                ldo = [P.dma("sync", xold[c][:, :T], st_in[c*128:(c+1)*128, t*T:(t+1)*T],
                             waits=last(xh_war if t == 0 else None,
                                        t_evac.get((0, t-1, 7)), t_evac.get((0, t-1, 6))))
                       for c in range(8)]
            for oc in range(8):
                bank = pb[4 + oc % 2]
                def dn(e2, oc=oc, bank=bank):
                    for fb in range(NFB):
                        i = e2.matmul(bank[:, :T], WB[:, DB0 + fb*1024 + oc*128:DB0 + fb*1024 + (oc+1)*128],
                                      rt[fb][:, :], start=(fb == 0), stop=(fb == NFB-1))
                    return i
                t_dn = P.ev("tensor", dn,
                            waits=last(t_wD, trlast,
                                       t_evac.get((e, t, oc-2)),
                                       t_evac.get((e, t-1, 6 + oc % 2)) if (oc < 2 and t > 0) else None,
                                       t_evac.get((e-1, NTILE-1, 6 + oc % 2)) if (oc < 2 and t == 0) else None))
                sc = scr[oc % 2]
                scwar = last(t_store.get((t, oc-2)),
                             t_store.get((t-1, 6 + oc % 2)) if (oc < 2 and t > 0) else None,
                             t_store.get((NTILE-1, 6 + oc % 2)) if (oc < 2 and t == 0) else None)
                if e == 0:
                    tev = P.ev("vector", lambda e2, oc=oc, bank=bank, sc=sc: e2.tensor_add(
                        sc[:, :T], bank[:, :T], xold[oc][:, :T]),
                        waits=last(t_dn, ldo[oc]) + scwar)
                else:
                    tev = P.ev("scalar", lambda e2, bank=bank, sc=sc: e2.copy(
                        sc[:, :T], bank[:, :T]), waits=last(t_dn) + scwar)
                t_evac[(e, t, oc)] = tev
                t_store[(t, oc)] = P.dma("gpsimd", st_out[oc*128:(oc+1)*128, t*T:(t+1)*T],
                                         sc[:, :T], waits=last(tev, t_store.get((t, oc))),
                                         accum=(e > 0))
            t_xt = t_up[(e, t, "u", NFB-1)]
    return t_store[(NTILE-1, 7)]



def emit_l0a(P, S, io, st_in, st_out, phase_dep):
    nc = P.nc
    C = S["C"]
    A, B, WB, pb = S["A"], S["B"], S["WB"], S["pb"]
    T = TTK
    SELPW0, SELQ0, WRE0, WIM0, OP0, IP0 = 0, 2048, 2560, 2688, 4096, 20480
    selpw = np.zeros((32, 2048), np.float32)
    selq = np.zeros((128, 512), np.float32)
    for i in range(16):
        for p in range(128):
            selpw[2*i + p//64, i*128 + p] = 1.0
            selq[p, i*32 + 2*i + p//64] = 1.0
    d_selpw = nc.inline_tensor(selpw, "l0a_selpw")
    d_selq = nc.inline_tensor(selq, "l0a_selq")
    tw = []
    tw.append(P.dma("gpsimd", WB[0:32, SELPW0:SELPW0+2048], d_selpw[:, :], waits=last(phase_dep)))
    tw.append(P.dma("gpsimd", WB[:, SELQ0:SELQ0+512], d_selq[:, :]))
    tw.append(P.dma("gpsimd", WB[:, WRE0:WRE0+128], io["l0a_wrebd"][:, :]))
    tw.append(P.dma("gpsimd", WB[:, WIM0:WIM0+128], io["l0a_wimbd"][:, :]))
    for h2 in range(2):
        tw.append(P.dma("gpsimd", WB[:, OP0+h2*8192:OP0+(h2+1)*8192].rearrange("p (a f) -> p a f", f=1024),
                        io["l0a_outproj"].rearrange("(a p) f -> p a f", p=128)[:, h2*8:(h2+1)*8, :]))
    for h2 in range(2):
        tw.append(P.dma("gpsimd", WB[:, IP0+h2*16512:IP0+(h2+1)*16512].rearrange("p (a f) -> p a f", f=4128),
                        io["l0a_inproj"].rearrange("(a p) f -> p a f", p=128)[:, h2*4:(h2+1)*4, :]))
    pcol, pcol2, misc, row1, carry = S["pcol"], S["pcol2"], S["misc"], S["row1"], S["carry"]
    t_cin = P.dma("sync", carry[:, 0:33], io["cin"][:, :], waits=last(phase_dep))
    tw.append(t_cin)
    tw.append(P.dma("sync", misc[:, 0:132], io["l0a_convk"][:, :]))
    tw.append(P.dma("sync", pcol2[:, 0:33], io["l0a_convb"][:, :]))
    tw.append(P.dma("sync", pcol[:, 0:16], io["l0a_theta"][:, :]))
    tw.append(P.dma("sync", pcol[:, 16:32], io["l0a_nsre"][:, :]))
    tw.append(P.dma("sync", pcol2[:, 36:52], io["l0a_nsim"][:, :]))
    tw.append(P.dma("sync", pcol2[0:32, 52:53], io["l0a_slopes"][:, :]))
    tw.append(P.dma("sync", pcol2[0:32, 35:36], io["l0a_sscale"][:, :]))
    tw.append(P.dma("sync", row1[0:1, 8:11], io["l0a_dlg"][:, :]))
    t1 = P.ev("scalar", lambda e: e.activation(pcol2[0:32, 53:54], pcol2[0:32, 52:53],
              AF.Abs), waits=tw)
    t2 = P.ev("scalar", lambda e: e.activation(pcol2[0:32, 53:54], pcol2[0:32, 53:54], AF.Exp,
              scale=-1.0), waits=last(t1))
    t3 = P.ev("vector", lambda e: e.tensor_single_scalar(pcol2[0:32, 53:54], pcol2[0:32, 53:54],
              1.0, ALU.add), waits=last(t2))
    t4 = P.ev("scalar", lambda e: e.activation(pcol2[0:32, 53:54], pcol2[0:32, 53:54], AF.Ln),
              waits=last(t3))
    t5 = P.ev("vector", lambda e: e.tensor_single_scalar(pcol2[0:32, 54:55], pcol2[0:32, 52:53],
              0.0, ALU.max), waits=last(t4))
    t6 = P.ev("vector", lambda e: e.tensor_add(pcol2[0:32, 53:54], pcol2[0:32, 53:54],
              pcol2[0:32, 54:55]), waits=last(t5))
    t_sp = P.ev("vector", lambda e: e.tensor_single_scalar(pcol2[0:32, 34:35], pcol2[0:32, 53:54],
               -1.0, ALU.mult), waits=last(t6))
    s1 = P.ev("vector", lambda e: e.reduce_max(row1[0:1, 11:12], row1[0:1, 8:11], AX.X), waits=tw)
    s2 = P.ev("vector", lambda e: e.tensor_scalar(row1[0:1, 8:11], row1[0:1, 8:11],
              row1[0:1, 11:12], None, ALU.subtract), waits=last(s1))
    s3 = P.ev("scalar", lambda e: e.activation(row1[0:1, 8:11], row1[0:1, 8:11], AF.Exp),
              waits=last(s2))
    s4 = P.ev("vector", lambda e: e.reduce_sum(row1[0:1, 11:12], row1[0:1, 8:11], AX.X),
              waits=last(s3))
    s5 = P.ev("vector", lambda e: e.reciprocal(row1[0:1, 12:13], row1[0:1, 11:12]), waits=last(s4))
    s6 = P.ev("vector", lambda e: e.tensor_scalar(row1[0:1, 8:11], row1[0:1, 8:11],
              row1[0:1, 12:13], None, ALU.mult), waits=last(s5))
    s7 = P.ev("tensor", lambda e: e.matmul(pb[2][:, 0:3], C.ones_row[0:1, :], row1[0:1, 8:11],
              start=True, stop=True), waits=last(s6, C.ready))
    t_wbc = P.ev("vector", lambda e: e.tensor_copy(pcol[:, 32:35], pb[2][:, 0:3]), waits=last(s7))
    S["ln_x"] = [view(A[c//2], c % 2) for c in range(8)]
    S["scr_f0"] = view(A[4], 0)
    S["ln_ab"] = view(A[5], 0)
    S["ln_bb"] = view(A[5], 1)
    xh_bf = [view(B[c//2], c % 2) for c in range(8)]
    zc = A[6]
    xv = view(A[7], 0)
    wph = view(A[7], 1)
    sint = view(A[8], 0)
    cost = view(A[8], 1)
    poly = view(A[9], 0)
    fsc = view(A[9], 1)
    sraw = view(A[10], 0)
    Dt = view(A[10], 1)
    pwt = view(A[11], 0)
    dent = view(A[11], 1)
    cst = view(A[12], 0)
    d2t = view(A[12], 1)
    tails = view(A[13], 0)
    oevac = [view(A[13], 1), view(A[4], 1)]
    nre = [B[4 + i//2][:, (i % 2)*T:(i % 2)*T + T] for i in range(16)]
    nim = [B[12 + i//2][:, (i % 2)*T:(i % 2)*T + T] for i in range(16)]
    sqbf = view(B[20], 0)
    pwbf = B[20][0:32, T:2*T]
    sscr = [view(B[21], 0), view(B[21], 1)]
    st = {"carry": {i: t_cin for i in range(34)}, "tail": {}, "zcuse": None,
          "zev": {}, "quse": None, "qrd": None, "bc": t_wbc, "lastbmm": s7,
          "y": None, "xvuse": None, "denuse": None, "dw": None, "arcf": None,
          "oev": None, "ost": None}
    def conv(ci, dst, n_rows, deps):
        k = lambda tap: misc[0:n_rows, 4*ci + tap:4*ci + tap + 1]
        bcol = pcol2[0:n_rows, ci:ci+1]
        tt1 = P.ev("vector", lambda e: e.tensor_scalar(dst, zc[0:n_rows, 0:T], k(0), bcol,
                   ALU.mult, ALU.add), waits=deps)
        tt2 = P.ev("vector", lambda e: e.scalar_tensor_tensor(dst, zc[0:n_rows, 1:1+T], k(1),
                   dst, ALU.mult, ALU.add), waits=last(tt1))
        tt3 = P.ev("vector", lambda e: e.scalar_tensor_tensor(dst, zc[0:n_rows, 2:2+T], k(2),
                   dst, ALU.mult, ALU.add), waits=last(tt2))
        return P.ev("vector", lambda e: e.scalar_tensor_tensor(dst, zc[0:n_rows, 3:3+T], k(3),
                   dst, ALU.mult, ALU.add), waits=last(tt3))
    def zchunk(ci, n_rows, bank_i, mm_deps):
        bank = pb[bank_i]
        def mm(e, ci=ci, bank=bank, n_rows=n_rows):
            for kc in range(8):
                i2 = e.matmul(bank[0:n_rows, 0:T], WB[:, IP0 + kc*4128 + ci*128:IP0 + kc*4128 + ci*128 + n_rows],
                              xh_bf[kc][:, :T], start=(kc == 0), stop=(kc == 7))
            return i2
        tmm = P.ev("tensor", mm, waits=last(st["zev"].get(bank_i)) + list(mm_deps))
        tev = P.ev("scalar", lambda e, bank=bank, n_rows=n_rows: e.copy(
            zc[0:n_rows, 3:3+T], bank[0:n_rows, 0:T]), waits=last(tmm, st["zcuse"]))
        st["zev"][bank_i] = tev
        tzt = P.ev("vector", lambda e, n_rows=n_rows, ci=ci: e.tensor_copy(
            zc[0:n_rows, 0:3], tails[0:n_rows, 3*ci:3*ci+3]), waits=last(tev, st["tail"][ci]))
        tnt = P.ev("vector", lambda e, n_rows=n_rows, ci=ci: e.tensor_copy(
            tails[0:n_rows, 3*ci:3*ci+3], zc[0:n_rows, T:T+3]), waits=last(tev))
        st["tail"][ci] = tnt
        return [tev, tzt, tnt]
    toks_ov, _ = emit_ln(P, S, st_in, 0, 3, xh_bf, None, dep=None)
    for ci in range(33):
        n_rows = 32 if ci == 32 else 128
        cols = 4096 if ci == 32 else (2048 + (ci-16)*128 if ci >= 16 else ci*128)
        def mmo(e, ci=ci, n_rows=n_rows, cols=cols):
            for kc in range(8):
                i2 = e.matmul(pb[0][0:n_rows, 0:3], WB[:, IP0 + kc*4128 + cols:IP0 + kc*4128 + cols + n_rows],
                              xh_bf[kc][:, :3], start=(kc == 0), stop=(kc == 7))
            return i2
        tmm = P.ev("tensor", mmo, waits=last(toks_ov[7], tw[-1], st["zev"].get(0)))
        tevo = P.ev("scalar", lambda e, ci=ci, n_rows=n_rows: e.copy(
            tails[0:n_rows, 3*ci:3*ci+3], pb[0][0:n_rows, 0:3]), waits=last(tmm))
        st["zev"][0] = tevo
        st["tail"][ci] = tevo
    if "dbg_tails" in io:
        P.dma("sync", io["dbg_tails"][:, 0:99], tails[:, 0:99], waits=last(st["zev"][0]))
        P.dma("gpsimd", io["dbg_xho"][:, 0:3], xh_bf[0][:, 0:3], waits=last(toks_ov[7]))
    S["ln_war"] = None
    st_tok = st["zev"][0]
    for t in range(NTILE):
        toks, _ = emit_ln(P, S, st_in, t*T + 3, T, xh_bf, None, dep=st_tok)
        ctok = zchunk(32, 32, 0, last(toks[7], t_sp))
        tcv = conv(32, sraw[0:32, 0:T], 32, ctok)
        st["zcuse"] = tcv
        tD = P.dma("sync", Dt[0:32, 0:T], io["l0a_dist"][:, t*T:(t+1)*T],
                   waits=last(st["qrd"], st["dbgq"] if "dbgq" in st else None))
        p1 = P.ev("vector", lambda e: e.tensor_scalar(pwt[0:32, 0:T], sraw[0:32, 0:T],
                  pcol2[0:32, 35:36], -20.0, ALU.mult, ALU.max), waits=last(tcv, st["denuse"]))
        p2 = P.ev("vector", lambda e: e.tensor_single_scalar(pwt[0:32, 0:T], pwt[0:32, 0:T],
                  20.0, ALU.min), waits=last(p1))
        p3 = P.ev("vector", lambda e: e.tensor_scalar(Dt[0:32, 0:T], Dt[0:32, 0:T],
                  pcol2[0:32, 34:35], None, ALU.mult), waits=last(tD, p2, t_sp))
        st["dw"] = None
        p4 = P.ev("vector", lambda e: e.tensor_add(pwt[0:32, 0:T], pwt[0:32, 0:T], Dt[0:32, 0:T]),
                  waits=last(p3))
        p5 = P.ev("scalar", lambda e: e.activation(pwt[0:32, 0:T], pwt[0:32, 0:T], AF.Exp),
                  waits=last(p4))
        if "dbg_pw" in io:
            P.dma("sync", io["dbg_sraw"][:, t*T:(t+1)*T], sraw[0:32, 0:T], waits=last(tcv))
            P.dma("sync", io["dbg_pw"][:, t*T:(t+1)*T], pwt[0:32, 0:T], waits=last(p5))
            P.dma("sync", io["dbg_xh"][:, t*T:(t+1)*T], S["ln_x"][0][:, :T], waits=last(toks[7]))
        d1 = P.ev("vector", lambda e: e.tensor_tensor_scan(dent[0:32, 0:T], pwt[0:32, 0:T],
                  pwt[0:32, 0:T], carry[0:32, 32:33], ALU.add, ALU.bypass),
                  waits=last(p5, st["carry"][32], st["denuse"]))
        st["carry"][32] = P.ev("vector", lambda e: e.tensor_copy(carry[0:32, 32:33],
                               dent[0:32, T-1:T]), waits=last(d1))
        if "dbg_den" in io:
            P.dma("sync", io["dbg_den"][:, t*T:(t+1)*T], dent[0:32, 0:T], waits=last(st["carry"][32]))
        tpb = P.ev("vector", lambda e: e.tensor_copy(pwbf, pwt[0:32, 0:T]),
                   waits=last(p5, st["lastbmm"]))
        for i in range(16):
            ctok = zchunk(i, 128, i % 2, last(toks[7]))
            tcv = conv(i, xv[:, 0:T], 128, ctok + last(st["xvuse"]))
            st["zcuse"] = tcv
            if "dbg_xv" in io and i == 0:
                P.dma("sync", io["dbg_xv"][:, t*T:(t+1)*T], xv[:, 0:T], waits=last(tcv))
            f1 = P.ev("vector", lambda e, i=i: e.tensor_scalar(wph[:, 0:T], xv[:, 0:T],
                      pcol[:, i:i+1], None, ALU.mult), waits=last(tcv))
            st["xvuse"] = f1
            if "dbg_phi" in io and i == 0:
                P.dma("sync", io["dbg_phi"][:, t*T:(t+1)*T], wph[:, 0:T], waits=last(f1))
            PI = float(np.pi)
            w1 = P.ev("vector", lambda e: e.tensor_single_scalar(sint[:, 0:T], wph[:, 0:T],
                      -PI, ALU.is_lt), waits=last(f1, st["bc"]))
            w2 = P.ev("vector", lambda e: e.tensor_single_scalar(cost[:, 0:T], wph[:, 0:T],
                      PI, ALU.is_gt), waits=last(w1))
            w3 = P.ev("vector", lambda e: e.tensor_sub(sint[:, 0:T], sint[:, 0:T], cost[:, 0:T]),
                      waits=last(w2))
            w4 = P.ev("vector", lambda e: e.scalar_tensor_tensor(wph[:, 0:T], sint[:, 0:T],
                      float(2*np.pi), wph[:, 0:T], ALU.mult, ALU.add), waits=last(w3))
            sn = P.ev("scalar", lambda e: e.activation(sint[:, 0:T], wph[:, 0:T], AF.Sin),
                      waits=last(w4))
            c1 = P.ev("scalar", lambda e: e.activation(cost[:, 0:T], wph[:, 0:T],
                      AF.Abs), waits=last(w4))
            c2 = P.ev("vector", lambda e: e.tensor_scalar(cost[:, 0:T], cost[:, 0:T],
                      -1.0, float(np.pi/2), ALU.mult, ALU.add), waits=last(c1))
            cs_ = P.ev("scalar", lambda e: e.activation(cost[:, 0:T], cost[:, 0:T], AF.Sin),
                       waits=last(c2, sn))
            q1 = P.ev("vector", lambda e: e.tensor_scalar(poly[:, 0:T], xv[:, 0:T],
                      pcol[:, 34:35], None, ALU.mult), waits=last(cs_, t_wbc))
            q2 = P.ev("vector", lambda e: e.tensor_scalar(poly[:, 0:T], poly[:, 0:T],
                      -1.0, pcol[:, 33:34], ALU.mult, ALU.add), waits=last(q1))
            q3 = P.ev("vector", lambda e: e.tensor_mul(poly[:, 0:T], poly[:, 0:T], xv[:, 0:T]),
                      waits=last(q2))
            q4 = P.ev("vector", lambda e: e.tensor_scalar(poly[:, 0:T], poly[:, 0:T],
                      1.0, pcol[:, 32:33], ALU.mult, ALU.add), waits=last(q3))
            r1 = P.ev("vector", lambda e: e.tensor_mul(cost[:, 0:T], cost[:, 0:T], poly[:, 0:T]),
                      waits=last(q4))
            r2 = P.ev("vector", lambda e: e.tensor_mul(sint[:, 0:T], sint[:, 0:T], poly[:, 0:T]),
                      waits=last(r1))
            bmm = P.ev("tensor", lambda e, i=i: e.matmul(pb[2][:, 0:T], WB[0:32, SELPW0+i*128:SELPW0+(i+1)*128],
                       pwbf, start=True, stop=True), waits=last(tpb, st["bc"]))
            st["lastbmm"] = bmm
            if "dbg_phi" in io and i == 0:
                P.dma("sync", io["dbg_sin"][:, t*T:(t+1)*T], sint[:, 0:T], waits=last(r2))
                P.dma("sync", io["dbg_cos"][:, t*T:(t+1)*T], cost[:, 0:T], waits=last(r2))
                tpwb = P.ev("vector", lambda e: e.tensor_copy(fsc[:, 0:T], pb[2][:, 0:T]),
                            waits=last(bmm))
                P.dma("sync", io["dbg_pwb"][:, t*T:(t+1)*T], fsc[:, 0:T], waits=last(tpwb))
            b1 = P.ev("vector", lambda e: e.tensor_mul(cost[:, 0:T], cost[:, 0:T], pb[2][:, 0:T]),
                      waits=last(bmm, r2))
            b2 = P.ev("vector", lambda e: e.tensor_mul(sint[:, 0:T], sint[:, 0:T], pb[2][:, 0:T]),
                      waits=last(b1))
            st["bc"] = b2
            if "dbg_rem" in io and i == 0:
                P.dma("sync", io["dbg_rem"][:, t*T:(t+1)*T], cost[:, 0:T], waits=last(b1))
            for (srcr, arch, coff) in ((cost, nre[i], i), (sint, nim[i], 16 + i)):
                sc1 = P.ev("vector", lambda e, srcr=srcr, coff=coff: e.tensor_tensor_scan(
                    fsc[:, 0:T], srcr[:, 0:T], srcr[:, 0:T], carry[:, coff:coff+1],
                    ALU.add, ALU.bypass), waits=last(b2, st["carry"][coff], st["arcf"]))
                st["carry"][coff] = P.ev("vector", lambda e, coff=coff: e.tensor_copy(
                    carry[:, coff:coff+1], fsc[:, T-1:T]), waits=last(sc1))
                qs = P.ev("scalar", lambda e: e.activation(sqbf[:, 0:T], fsc[:, 0:T], AF.Square),
                          waits=last(sc1, st["quse"]))
                qmm = P.ev("tensor", lambda e, i=i: e.matmul(pb[3][0:32, 0:T],
                           WB[:, SELQ0+i*32:SELQ0+(i+1)*32], sqbf[:, 0:T],
                           start=True, stop=True),
                           waits=last(qs, st["quse"],
                                      st["qrd"] if (i == 0 and srcr is cost) else None))
                if i == 0 and srcr is cost:
                    qadd = P.ev("vector", lambda e: e.tensor_copy(Dt[0:32, 0:T], pb[3][0:32, 0:T]),
                                waits=last(qmm))
                else:
                    qadd = P.ev("vector", lambda e: e.scalar_tensor_tensor(Dt[0:32, 0:T],
                                pb[3][0:32, 0:T], 1.0, Dt[0:32, 0:T], ALU.mult, ALU.add),
                                waits=last(qmm))
                st["quse"] = qadd
                st["arcf"] = P.ev("vector", lambda e, arch=arch: e.tensor_copy(
                    arch, fsc[:, 0:T]), waits=last(qs))
                if "dbg_nre" in io and i == 0 and srcr is cost:
                    P.dma("sync", io["dbg_nre"][:, t*T:(t+1)*T], fsc[:, 0:T], waits=last(st["arcf"]))
        e1 = P.ev("vector", lambda e: e.tensor_single_scalar(dent[0:32, 0:T], dent[0:32, 0:T],
                  1e-4, ALU.max), waits=last(st["carry"][32]))
        st["denuse"] = e1
        e2 = P.ev("vector", lambda e: e.tensor_mul(d2t[0:32, 0:T], dent[0:32, 0:T],
                  dent[0:32, 0:T]), waits=last(e1))
        e3 = P.ev("vector", lambda e: e.tensor_single_scalar(d2t[0:32, 0:T], d2t[0:32, 0:T],
                  1e-5, ALU.mult), waits=last(e2))
        if "dbg_q" in io:
            P.dma("sync", io["dbg_q"][:, t*T:(t+1)*T], Dt[0:32, 0:T], waits=last(st["quse"]))
        e4 = P.ev("vector", lambda e: e.scalar_tensor_tensor(cst[0:32, 0:T], Dt[0:32, 0:T],
                  1.0/128.0, d2t[0:32, 0:T], ALU.mult, ALU.add),
                  waits=last(e3, st["quse"]))
        st["qrd"] = e4
        if "dbg_s2" in io:
            tds = P.ev("vector", lambda e: e.tensor_copy(d2t[0:32, 0:T], cst[0:32, 0:T]),
                       waits=last(e4))
            P.dma("sync", io["dbg_s2"][:, t*T:(t+1)*T], d2t[0:32, 0:T], waits=last(tds))
        e5 = P.ev("scalar", lambda e: e.activation(cst[0:32, 0:T], cst[0:32, 0:T], AF.Ln),
                  waits=last(e4))
        e6 = P.ev("scalar", lambda e: e.activation(cst[0:32, 0:T], cst[0:32, 0:T], AF.Exp,
                  scale=-0.5), waits=last(e5))
        if "dbg_cs" in io:
            P.dma("sync", io["dbg_cs"][:, t*T:(t+1)*T], cst[0:32, 0:T], waits=last(e6))
        tcb = P.ev("vector", lambda e: e.tensor_copy(pwbf, cst[0:32, 0:T]),
                   waits=last(e6, st["lastbmm"]))
        t_dbgn = None
        if "dbg_nre2" in io:
            t_dbgn = P.dma("gpsimd", io["dbg_nre2"][:, t*T:(t+1)*T], nre[0][:, :], waits=last(st["arcf"]))
            t_dbgn = P.dma("gpsimd", io["dbg_nim2"][:, t*T:(t+1)*T], nim[0][:, :], waits=last(st["arcf"]))
        for i in range(16):
            bmm = P.ev("tensor", lambda e, i=i: e.matmul(pb[2][:, 0:T], WB[0:32, SELPW0+i*128:SELPW0+(i+1)*128],
                       pwbf, start=True, stop=True), waits=last(tcb, st["bc"], t_dbgn if i == 0 else None))
            st["lastbmm"] = bmm
            if "dbg_csb" in io and i == 0:
                tcsb = P.ev("vector", lambda e: e.tensor_copy(fsc[:, 0:T], pb[2][:, 0:T]),
                            waits=last(bmm))
                P.dma("sync", io["dbg_csb"][:, t*T:(t+1)*T], fsc[:, 0:T], waits=last(tcsb))
            g1 = P.ev("vector", lambda e, i=i: e.scalar_tensor_tensor(sscr[0][:, :T], nre[i],
                      pcol[:, 16+i:17+i], pb[2][:, 0:T], ALU.mult, ALU.mult),
                      waits=last(bmm, st["arcf"]))
            g2 = P.ev("vector", lambda e, i=i: e.scalar_tensor_tensor(sscr[1][:, :T], nim[i],
                      pcol2[:, 36+i:37+i], pb[2][:, 0:T], ALU.mult, ALU.mult), waits=last(g1))
            st["bc"] = g2
            ymm1 = P.ev("tensor", lambda e: e.matmul(pb[4][:, 0:T], WB[:, WRE0:WRE0+128],
                        sscr[0][:, :T], start=True, stop=False), waits=last(g1, st["y"]))
            ymm = P.ev("tensor", lambda e: e.matmul(pb[4][:, 0:T], WB[:, WIM0:WIM0+128],
                       sscr[1][:, :T], start=False, stop=True), waits=last(g2))
            ctok = zchunk(16 + i, 128, i % 2, last(toks[7]))
            tcv = conv(16 + i, xv[:, 0:T], 128, ctok + last(st["xvuse"]))
            st["zcuse"] = tcv
            tsl = P.ev("scalar", lambda e: e.activation(xv[:, 0:T], xv[:, 0:T], AF.Silu),
                       waits=last(tcv))
            if "dbg_gate" in io and i == 0:
                P.dma("sync", io["dbg_gate"][:, t*T:(t+1)*T], xv[:, 0:T], waits=last(tsl))
            tgd = P.ev("vector", lambda e, i=i: e.tensor_mul(nre[i], pb[4][:, 0:T], xv[:, 0:T]),
                       waits=last(ymm, tsl, st["arcf"]))
            st["xvuse"] = tgd
            st["y"] = tgd
            st[("gated", i)] = tgd
            if "dbg_gtd" in io and i == 0:
                P.dma("gpsimd", io["dbg_gtd"][:, t*T:(t+1)*T], nre[0][:, :], waits=last(tgd))
        for oc in range(8):
            def omm(e, oc=oc):
                for i in range(16):
                    r = e.matmul(pb[5][:, 0:T], WB[:, OP0 + i*1024 + oc*128:OP0 + i*1024 + (oc+1)*128],
                                 nre[i], start=(i == 0), stop=(i == 15))
                return r
            tom = P.ev("tensor", omm, waits=last(st[("gated", 15)], st["oev"]))
            sc = oevac[oc % 2]
            tev = P.ev("vector", lambda e, sc=sc, oc=oc: e.tensor_add(sc[:, :T], pb[5][:, 0:T],
                       S["ln_x"][oc][:, :T]), waits=last(tom, S.get(("l0a_ost", oc % 2))))
            st["oev"] = tev
            st_tok = P.dma("scalar", st_out[oc*128:(oc+1)*128, t*T:(t+1)*T], sc[:, :T],
                           waits=last(tev))
            S[("l0a_ost", oc % 2)] = st_tok
        S["ln_war"] = st_tok
    P.dma("scalar", io["cout"][:, :], carry[:, 0:33], waits=last(st["carry"][32],
          st["carry"][15], st["carry"][31]))
    return st_tok



def emit_l1a(P, S, io, st_in, st_out, phase_dep):
    nc = P.nc
    C = S["C"]
    A, B, WB, pb = S["A"], S["B"], S["WB"], S["pb"]
    T = TTK
    pcol, pcol2 = S["pcol"], S["pcol2"]
    hcol = S["hcol"]
    tw = []
    for h2 in range(2):
        tw.append(P.dma("gpsimd", WB[:, h2*8192:(h2+1)*8192].rearrange("p (a f) -> p a f", f=1024),
                        io["l1a_wout"].rearrange("(a p) f -> p a f", p=128)[:, h2*8:(h2+1)*8, :],
                        waits=last(phase_dep)))
    t_om = P.dma("sync", pcol2[:, 0:16], io["l1a_omega"][:, :], waits=last(phase_dep))
    t_lm = P.dma("sync", pcol2[:, 20:21], io["l1a_lmask"][:, :])
    t_hin = P.dma("sync", hcol[:, 0:16], io["hin"][:, :])
    c1 = P.ev("scalar", lambda e: e.activation(pcol[:, 0:16], pcol2[:, 0:16], AF.Abs),
              waits=last(t_om, t_hin, t_lm, tw[-1]))
    c2 = P.ev("scalar", lambda e: e.activation(pcol[:, 0:16], pcol[:, 0:16], AF.Exp, scale=-1.0),
              waits=last(c1))
    c3 = P.ev("vector", lambda e: e.tensor_single_scalar(pcol[:, 0:16], pcol[:, 0:16], 1.0,
              ALU.add), waits=last(c2))
    c4 = P.ev("scalar", lambda e: e.activation(pcol[:, 0:16], pcol[:, 0:16], AF.Ln), waits=last(c3))
    c5 = P.ev("vector", lambda e: e.tensor_single_scalar(pcol[:, 16:32], pcol2[:, 0:16], 0.0,
              ALU.max), waits=last(c4))
    c6 = P.ev("vector", lambda e: e.tensor_add(pcol[:, 0:16], pcol[:, 0:16], pcol[:, 16:32]),
              waits=last(c5))
    t_coef = P.ev("vector", lambda e: e.tensor_single_scalar(pcol[:, 0:16], pcol[:, 0:16], -8.0,
              ALU.mult), waits=last(c6))
    S["ln_x"] = [view(A[c//2], c % 2) for c in range(8)]
    S["scr_f0"] = view(A[4], 0)
    S["ln_ab"] = view(A[5], 0)
    S["ln_bb"] = view(A[5], 1)
    GT = 4
    xg = lambda c, tig: B[c + 8*(tig // 2)][:, (tig % 2)*T:(tig % 2)*T + T]
    evb = [view(B[16], 0), view(B[16], 1)]
    evf = [view(A[6], 0), view(A[6], 1)]
    tok = {"we": [None, None], "slot": {}, "bank": {}}
    dep = phase_dep
    for g in range(NT // (GT*T)):
        lnt = None
        for tig in range(GT):
            xh_bf = [xg(c, tig) for c in range(8)]
            toks, _ = emit_ln(P, S, st_in, (g*GT + tig)*T, T, xh_bf, None, dep=dep)
            dep = toks[7]
            lnt = toks[7]
        for oc in range(32):
            wbuf = S["w1"] if oc % 2 == 0 else S["w2"]
            t_wl = P.dma("gpsimd", wbuf[:, 0:1024].rearrange("p (a f) -> p a f", f=128),
                         io["l1a_win"].rearrange("(a p) f -> p a f", p=128)[:, :, oc*128:(oc+1)*128],
                         waits=last(tok["we"][oc % 2]), ctr="wld")
            for tig in range(GT):
                bank = pb[tig % 2]
                def mm(e, tig=tig, bank=bank, wbuf=wbuf):
                    for kc in range(8):
                        i2 = e.matmul(bank[:, 0:T], wbuf[:, kc*128:(kc+1)*128], xg(kc, tig),
                                      start=(kc == 0), stop=(kc == 7))
                    return i2
                tmm = P.ev("tensor", mm, waits=last(t_wl, lnt, tok["bank"].get(tig % 2)))
                tok["we"][oc % 2] = tmm
                col = (g*GT + tig)*T
                sl = tig % 2
                if oc < 16:
                    tev = P.ev("vector", lambda e, bank=bank, sl=sl: e.tensor_copy(
                        evb[sl][:, :T], bank[:, 0:T]), waits=last(tmm, tok["slot"].get(("b", sl))))
                    td = P.dma("scalar", io["xi_b"][oc*128:(oc+1)*128, col:col+T], evb[sl][:, :T],
                               waits=last(tev))
                    tok["slot"][("b", sl)] = td
                else:
                    tsl = P.ev("scalar", lambda e, bank=bank, sl=sl: e.activation(
                        evb[sl][:, :T], bank[:, 0:T], AF.Silu),
                        waits=last(tmm, tok["slot"].get(("b", sl))))
                    td = P.dma("scalar", io["sz_b"][(oc-16)*128:(oc-15)*128, col:col+T],
                               evb[sl][:, :T], waits=last(tsl))
                    tok["slot"][("b", sl)] = td
                    tev = tsl
                tok["bank"][tig % 2] = tev
    phA = td
    xig = lambda c, tig: B[c][:, (tig % 2)*T:(tig % 2)*T + T]
    t_lastmm = phA
    for g in range(NT // (2*T)):
        xl = None
        for c in range(16):
            for tig in range(2):
                xl = P.dma("sync", xig(c, tig),
                           io["xi_b"][c*128:(c+1)*128, (g*2+tig)*T:(g*2+tig+1)*T],
                           waits=last(phA if g == 0 else tok["we"][0],
                                      tok["we"][1] if g > 0 else None))
        for oc in range(16):
            t_wl = P.dma("gpsimd", S["w1"][:, 0:2048].rearrange("p (a f) -> p a f", f=128),
                         io["l1a_wl"].rearrange("(a p) f -> p a f", p=128)[:, :, oc*128:(oc+1)*128],
                         waits=last(tok["we"][0]), ctr="wld")
            t_wv = P.dma("gpsimd", S["w2"][:, 0:2048].rearrange("p (a f) -> p a f", f=128),
                         io["l1a_wv"].rearrange("(a p) f -> p a f", p=128)[:, :, oc*128:(oc+1)*128],
                         waits=last(tok["we"][1]), ctr="wld")
            for tig in range(2):
                bank = pb[tig % 2]
                bank2 = pb[2 + tig % 2]
                def mml(e, tig=tig, bank=bank):
                    for kc in range(16):
                        i2 = e.matmul(bank[:, 0:T], S["w1"][:, kc*128:(kc+1)*128], xig(kc, tig),
                                      start=(kc == 0), stop=(kc == 15))
                    return i2
                def mmv(e, tig=tig, bank2=bank2):
                    for kc in range(16):
                        i2 = e.matmul(bank2[:, 0:T], S["w2"][:, kc*128:(kc+1)*128], xig(kc, tig),
                                      start=(kc == 0), stop=(kc == 15))
                    return i2
                tml = P.ev("tensor", mml, waits=last(t_wl, xl, tok["bank"].get(("l", tig))))
                tmv = P.ev("tensor", mmv, waits=last(t_wv, tok["bank"].get(("v", tig))))
                tok["we"][0] = tml
                tok["we"][1] = tmv
                col = (g*2 + tig)*T
                tep = P.ev("scalar", lambda e, bank=bank, tig=tig: e.copy(evf[tig][:, :T],
                           bank[:, 0:T]), waits=last(tml, tok["slot"].get(("f", tig))))
                tdp = P.dma("scalar", io["pre_f"][oc*128:(oc+1)*128, col:col+T], evf[tig][:, :T],
                            waits=last(tep))
                tok["slot"][("f", tig)] = tdp
                tok["bank"][("l", tig)] = tep
                tev2 = P.ev("vector", lambda e, bank2=bank2, tig=tig: e.tensor_copy(
                    evb[tig][:, :T], bank2[:, 0:T]), waits=last(tmv, tok["slot"].get(("b", tig))))
                tdv = P.dma("scalar", io["v_b"][oc*128:(oc+1)*128, col:col+T], evb[tig][:, :T],
                            waits=last(tev2))
                tok["slot"][("b", tig)] = tdv
                tok["bank"][("v", tig)] = tev2
    phB = tdv
    return emit_l1a_chain(P, S, io, st_in, st_out, phB, t_coef)


def emit_l1a_chain(P, S, io, st_in, st_out, phase_dep, t_coef):
    nc = P.nc
    C = S["C"]
    A, B, WB, pb = S["A"], S["B"], S["WB"], S["pb"]
    R = 128
    NBLK = NT // R
    pcol, pcol2, hcol = S["pcol"], S["pcol2"], S["hcol"]
    Pst = lambda c: A[c // 3][:, (c % 3)*132:(c % 3)*132 + 132]
    iPs = A[6][:, 0:132]
    tds = A[6][:, 132:260]
    lms = A[6][:, 260:388]
    hsc = A[7][:, 0:128]
    yts = A[7][:, 128:256]
    sqs = A[7][:, 256:384]
    td2 = A[7][:, 384:512]
    STs = A[8][:, 0:128]
    ivb = A[8][:, 128:256]
    esc = A[8][:, 256:384]
    csb = A[8][:, 384:385]
    rsm = A[8][:, 385:386]
    rvs = A[8][:, 400:401]
    gsb = A[8][:, 416:420]
    grow = A[9][0:1, 0:128]
    Wsl = A[9][:, 128:160]
    Wtl = A[9][:, 160:192]
    Xsl = A[9][:, 192:224]
    XTl = lambda sb: A[9][:, 224 + sb*32:224 + (sb+1)*32]
    vvrow = A[9][0:1, 352:480]
    ivrow = A[9][0:1, 480:512]
    ivr = A[13][0:1, 0:128]
    xos = [A[10][:, (k % 4)*128:(k % 4)*128 + 128] for k in range(4)] + \
          [A[11][:, (k % 4)*128:(k % 4)*128 + 128] for k in range(4)]
    oev = [A[12][:, 0:128], A[12][:, 128:256]]
    abf = lambda c: B[0 + c//4][:, (c % 4)*128:(c % 4)*128 + 128]
    ubf = lambda c: B[4 + c//4][:, (c % 4)*128:(c % 4)*128 + 128]
    vbf = lambda c: B[8 + c//4][:, (c % 4)*128:(c % 4)*128 + 128]
    ybf = lambda c: B[12 + c//4][:, (c % 4)*128:(c % 4)*128 + 128]
    zbf = lambda c: B[16 + c//4][:, (c % 4)*128:(c % 4)*128 + 128]
    vq = [B[20][:, 0:128], B[20][:, 128:256]]
    xiq = [B[20][:, 256:384], B[20][:, 384:512]]
    hbb = B[21][:, 0:16]
    tok = {}
    tmem = []
    for c in range(16):
        tmem.append(P.ev("gpsimd", lambda e, c=c: e.memset(Pst(c)[:, 0:1], 1.0),
                         waits=last(phase_dep if c == 0 else None)))
    tmem.append(P.ev("gpsimd", lambda e: e.memset(iPs[:, 0:1], 1.0)))
    t_hb = P.ev("vector", lambda e: e.tensor_copy(hbb[:, :], hcol[:, 0:16]),
                waits=last(phase_dep) + tmem)
    for c in range(16):
        tok[("hc", c)] = t_hb
    st_tok = None
    for b in range(NBLK):
        col = b*R
        for c in range(16):
            ldv = P.dma("sync", vq[c % 2][:, :], io["v_b"][c*128:(c+1)*128, col:col+R],
                        waits=last(tok.get(("vq", c % 2)), phase_dep if b == 0 else None))
            tsq = P.ev("scalar", lambda e, c=c: e.activation(sqs[:, :], vq[c % 2][:, :], AF.Square),
                       waits=last(ldv, tok.get("vvmm")))
            def vmm(e, c=c):
                return e.matmul(pb[5][0:1, 0:128], C.ones_col[:, :], sqs[:, :],
                                start=(c == 0), stop=(c == 15))
            tok["vvmm"] = P.ev("tensor", vmm, waits=last(tsq, tok.get("ivbdone") if c == 0 else None))
            tok[("vq", c % 2)] = tok["vvmm"]
        tvv = P.ev("vector", lambda e: e.tensor_copy(vvrow[:, :], pb[5][0:1, 0:128]),
                   waits=last(tok["vvmm"]))
        tiv = P.ev("vector", lambda e: e.reciprocal(ivr[:, :], vvrow[:, :]), waits=last(tvv))
        tib = P.ev("tensor", lambda e: e.matmul(pb[5][:, 0:128], C.ones_row[0:1, :], ivr[:, :],
                   start=True, stop=True), waits=last(tiv))
        tok["ivbdone"] = P.ev("vector", lambda e: e.tensor_copy(ivb[:, :], pb[5][:, 0:128]),
                              waits=last(tib))
        for c in range(16):
            ldp = P.dma("sync", lms[:, :], io["pre_f"][c*128:(c+1)*128, col:col+R],
                        waits=last(tok.get("lmuse")))
            ldv = P.dma("sync", vq[c % 2][:, :], io["v_b"][c*128:(c+1)*128, col:col+R],
                        waits=last(tok.get(("vq2", c % 2))))
            ldx = P.dma("sync", xiq[c % 2][:, :], io["xi_b"][c*128:(c+1)*128, col:col+R],
                        waits=last(tok.get(("xq", c % 2))))
            ldz = P.dma("gpsimd", zbf(c), io["sz_b"][c*128:(c+1)*128, col:col+R],
                        waits=last(tok.get(("zw", c))), ctr="zld")
            tok[("zw", c)] = ldz
            t1 = P.ev("scalar", lambda e: e.activation(lms[:, :], lms[:, :], AF.Sigmoid),
                      waits=last(ldp))
            t2 = P.ev("vector", lambda e, c=c: e.tensor_scalar(lms[:, :], lms[:, :],
                      pcol[:, c:c+1], None, ALU.mult), waits=last(t1, t_coef))
            t3 = P.ev("scalar", lambda e: e.activation(lms[:, :], lms[:, :], AF.Exp),
                      waits=last(t2))
            tok["lmuse"] = t3
            if b == 0:
                t3 = P.ev("vector", lambda e: e.tensor_scalar(lms[:, 0:1], lms[:, 0:1],
                          pcol2[:, 20:21], None, ALU.mult), waits=last(t3))
                tok["lmuse"] = t3
            tps = P.ev("vector", lambda e, c=c: e.tensor_tensor_scan(Pst(c)[:, 1:129], lms[:, :],
                      lms[:, :], 1.0, ALU.mult, ALU.bypass), waits=last(t3, tok.get(("puse", c))))
            tip = P.ev("vector", lambda e, c=c: e.reciprocal(iPs[:, 1:129], Pst(c)[:, 1:129]),
                       waits=last(tps))
            ttd = P.ev("vector", lambda e: e.tensor_sub(tds[:, :], iPs[:, 1:129], iPs[:, 0:128]),
                       waits=last(tip))
            tu = P.ev("vector", lambda e, c=c: e.tensor_mul(ubf(c), tds[:, :], xiq[c % 2][:, :]),
                      waits=last(ttd, ldx, tok.get(("ub", c))))
            tok[("xq", c % 2)] = tu
            ta = P.ev("vector", lambda e, c=c: e.tensor_mul(abf(c), vq[c % 2][:, :], iPs[:, 0:128]),
                      waits=last(ldv, tip, tok.get(("ab", c))))
            tt2 = P.ev("vector", lambda e, c=c: e.tensor_mul(td2[:, :], vq[c % 2][:, :],
                       Pst(c)[:, 0:128]), waits=last(ldv, tps))
            tok[("vq2", c % 2)] = tt2
            tvt = P.ev("vector", lambda e, c=c: e.scalar_tensor_tensor(vbf(c), td2[:, :], -2.0,
                       ivb[:, :], ALU.mult, ALU.mult), waits=last(tt2, tok["ivbdone"],
                       tok.get(("vb", c))))
            tok[("a", c)] = ta
            tok[("u", c)] = tu
            tok[("v", c)] = tvt
        def mmS(e):
            for kc in range(16):
                i2 = e.matmul(pb[0][:, 0:128], abf(kc), vbf(kc), start=(kc == 0), stop=(kc == 15))
            return i2
        tS = P.ev("tensor", mmS, waits=last(tok[("a", 15)], tok[("v", 15)], tok.get("STuse")))
        tSm = P.ev("vector", lambda e: e.tensor_mul(STs[:, :], pb[0][:, 0:128], C.sut[:, :]),
                   waits=last(tS))
        def mmE(e):
            for kc in range(16):
                i2 = e.matmul(pb[1][:, 0:128], vbf(kc), ubf(kc), start=(kc == 0), stop=(kc == 15))
            return i2
        tE = P.ev("tensor", mmE, waits=last(tok[("u", 15)], tok[("v", 15)], tok.get("esuse")))
        tEm = P.ev("vector", lambda e: e.tensor_mul(esc[:, :], pb[1][:, 0:128], C.slt[:, :]),
                   waits=last(tE))
        trs = P.ev("vector", lambda e: e.reduce_sum(rsm[:, :], esc[:, :], AX.X), waits=last(tEm))
        tok["esuse"] = trs
        def mmM(e):
            for kc in range(16):
                i2 = e.matmul(pb[2][:, 0:1], vbf(kc), hbb[:, kc:kc+1], start=(kc == 0),
                              stop=(kc == 15))
            return i2
        tM = P.ev("tensor", mmM, waits=last(t_hb, tok[("v", 15)], tok.get("cuse")))
        tc = P.ev("vector", lambda e: e.scalar_tensor_tensor(csb[:, :], pb[2][:, 0:1], 1.0,
                  rsm[:, :], ALU.mult, ALU.add), waits=last(tM, trs))
        tct = P.ev("tensor", lambda e: e.transpose(pb[4][0:1, 0:128], csb[:, :], C.ident[:, :]),
                   waits=last(tc, tok.get("growuse")))
        tcr = P.ev("vector", lambda e: e.tensor_copy(grow[:, :], pb[4][0:1, 0:128]),
                   waits=last(tct))
        for sb in range(4):
            def mmD(e, sb=sb):
                for kc in range(16):
                    i2 = e.matmul(pb[3][0:32, 0:32], abf(kc)[:, 32*sb:32*sb+32],
                                  vbf(kc)[:, 32*sb:32*sb+32], start=(kc == 0), stop=(kc == 15))
                return i2
            tD = P.ev("tensor", mmD, waits=last(tok[("a", 15)], tok[("v", 15)], tok.get("Wuse")))
            tW = P.ev("vector", lambda e: e.tensor_mul(Wsl[0:32, :], pb[3][0:32, 0:32],
                      C.sut[0:32, 0:32]), waits=last(tD))
            tWt0 = P.ev("tensor", lambda e: e.transpose(pb[3][0:32, 32:64], Wsl[0:32, :],
                        C.ident[0:32, 0:32]), waits=last(tW))
            tWt = P.ev("vector", lambda e: e.tensor_copy(Wtl[0:32, :], pb[3][0:32, 32:64]),
                       waits=last(tWt0))
            tX = P.ev("vector", lambda e: e.tensor_add(Xsl[0:32, :], Wsl[0:32, :],
                      C.eye32[:, :]), waits=last(tW))
            tok["Wuse"] = tX
            for it in range(4):
                tq = P.ev("tensor", lambda e: e.matmul(pb[3][0:32, 64:96], Wtl[0:32, :],
                          Wsl[0:32, :], start=True, stop=True), waits=last(tWt, tX))
                tWn = P.ev("vector", lambda e: e.tensor_copy(Wsl[0:32, :], pb[3][0:32, 64:96]),
                           waits=last(tq))
                tWt0 = P.ev("tensor", lambda e: e.transpose(pb[3][0:32, 32:64], Wsl[0:32, :],
                            C.ident[0:32, 0:32]), waits=last(tWn))
                tWt = P.ev("vector", lambda e: e.tensor_copy(Wtl[0:32, :], pb[3][0:32, 32:64]),
                           waits=last(tWt0))
                txa = P.ev("tensor", lambda e: e.matmul(pb[3][0:32, 96:128], Wtl[0:32, :],
                           Xsl[0:32, :], start=True, stop=True), waits=last(tWt))
                tX = P.ev("vector", lambda e: e.scalar_tensor_tensor(Xsl[0:32, :],
                          pb[3][0:32, 96:128], 1.0, Xsl[0:32, :], ALU.mult, ALU.add),
                          waits=last(txa))
            tXT0 = P.ev("tensor", lambda e: e.transpose(pb[3][0:32, 32:64], Xsl[0:32, :],
                        C.ident[0:32, 0:32]), waits=last(tX))
            tok["Wuse"] = P.ev("vector", lambda e, sb=sb: e.tensor_copy(XTl(sb)[0:32, :],
                               pb[3][0:32, 32:64]), waits=last(tXT0))
        tgp = None
        grow2 = A[13][0:1, 128:256]
        gful = A[13][:, 256:257]
        for sb in range(4):
            tt0 = P.ev("tensor", lambda e, sb=sb: e.transpose(pb[2][0:32, 32:33],
                       grow[0:1, 32*sb:32*sb+32], C.ident[0:1, 0:1]),
                       waits=last(tcr, tok.get("ruse")))
            if sb > 0:
                def mrest(e, sb=sb):
                    i2 = None
                    for sb2 in range(sb):
                        i2 = e.matmul(pb[2][0:32, 32:33], STs[32*sb2:32*sb2+32, 32*sb:32*sb+32],
                                      gful[32*sb2:32*sb2+32, 0:1],
                                      start=False, stop=(sb2 == sb-1), skip_group_check=True)
                    return i2
                tmr = P.ev("tensor", mrest, waits=last(tt0, tgp, tSm))
            else:
                tmr = tt0
            trv = P.ev("vector", lambda e: e.tensor_copy(rvs[0:32, :], pb[2][0:32, 32:33]),
                       waits=last(tmr))
            tgv = P.ev("tensor", lambda e, sb=sb: e.matmul(pb[2][0:32, 34:35], XTl(sb)[0:32, :],
                       rvs[0:32, :], start=True, stop=True), waits=last(trv, tok["Wuse"]))
            tgw = P.ev("vector", lambda e: e.tensor_copy(rvs[0:32, :], pb[2][0:32, 34:35]),
                       waits=last(tgv))
            tgt = P.ev("tensor", lambda e, sb=sb: e.transpose(pb[4][0:1, 32*sb:32*sb+32],
                       rvs[0:32, :], C.ident[0:32, 0:32]),
                       waits=last(tgw, tok.get("growuse") if sb == 0 else None))
            tgr2 = P.ev("vector", lambda e, sb=sb: e.tensor_copy(grow2[0:1, 32*sb:32*sb+32],
                        pb[4][0:1, 32*sb:32*sb+32]), waits=last(tgt))
            if sb < 3:
                tpf = P.ev("tensor", lambda e, sb=sb: e.matmul(pb[2][:, 36:37],
                           C.plc[:, sb*128:(sb+1)*128], rvs[0:32, :], start=True, stop=True),
                           waits=last(tgw, tok.get("gfev")))
                if sb == 0:
                    tgf = P.ev("vector", lambda e: e.tensor_copy(gful[:, 0:1],
                               pb[2][:, 36:37]), waits=last(tpf))
                else:
                    tgf = P.ev("vector", lambda e: e.scalar_tensor_tensor(gful[:, 0:1],
                               pb[2][:, 36:37], 1.0, gful[:, 0:1], ALU.mult, ALU.add),
                               waits=last(tpf))
                tok["gfev"] = tgf
            else:
                tgf = tgr2
            tok["ruse"] = tgf
            tgp = tgf
        tok["cuse"] = tgp
        tgb = P.ev("tensor", lambda e: e.matmul(pb[4][:, 0:128], C.ones_row[0:1, :],
                   grow2[:, :], start=True, stop=True), waits=last(tgr2, tgp))
        tok["growuse"] = tgb
        t_y = {}
        for c in range(16):
            twt = P.ev("vector", lambda e, c=c: e.scalar_tensor_tensor(tds[:, :], abf(c), 1.0,
                       pb[4][:, 0:128], ALU.mult, ALU.mult), waits=last(tgb, tok.get("scanuse")))
            tok[("ab", c)] = twt
            tsc2 = P.ev("vector", lambda e, c=c: e.tensor_tensor_scan(hsc[:, :], tds[:, :],
                        ubf(c), hcol[:, c:c+1], ALU.add, ALU.add),
                        waits=last(twt, tok.get(("hc", c))))
            tok[("ub", c)] = tsc2
            the = P.ev("vector", lambda e, c=c: e.tensor_mul(hcol[:, c:c+1], hsc[:, 127:128],
                       Pst(c)[:, 128:129]), waits=last(tsc2))
            tok[("hc", c)] = the
            th2 = P.ev("vector", lambda e, c=c: e.tensor_mul(yts[:, :], hsc[:, :],
                       Pst(c)[:, 1:129]), waits=last(the))
            tok["scanuse"] = th2
            tok[("puse", c)] = th2
            ty = P.ev("vector", lambda e, c=c: e.tensor_mul(ybf(c), yts[:, :], zbf(c)),
                      waits=last(th2, tok[("zw", c)], tok.get(("yw", c))))
            tok[("zw", c)] = ty
            t_y[c] = ty
        tok["growuse"] = tok[("ab", 15)]
        tok["STuse"] = tok["cuse"]
        t_hb = P.ev("vector", lambda e: e.tensor_copy(hbb[:, :], hcol[:, 0:16]),
                    waits=last(tok[("hc", 15)], tok.get("cuse")))
        ldo = [P.dma("sync", xos[c][:, :], st_in[c*128:(c+1)*128, col:col+R],
                     waits=last(tok.get(("xo", c)))) for c in range(8)]
        for oc in range(8):
            bank = pb[6 + oc % 2]
            def omm(e, oc=oc, bank=bank):
                for kc in range(16):
                    i2 = e.matmul(bank[:, 0:128], WB[:, kc*1024 + oc*128:kc*1024 + (oc+1)*128],
                                  ybf(kc), start=(kc == 0), stop=(kc == 15))
                return i2
            tom = P.ev("tensor", omm, waits=last(t_y[15], tok.get(("oev", oc % 2))))
            for c in range(16):
                tok[("yw", c)] = tom
            sc = oev[oc % 2]
            tev = P.ev("vector", lambda e, oc=oc, bank=bank, sc=sc: e.tensor_add(sc[:, :],
                       bank[:, 0:128], xos[oc][:, :]), waits=last(tom, ldo[oc],
                       tok.get(("ost", oc % 2))))
            tok[("oev", oc % 2)] = tev
            tok[("xo", oc)] = tev
            st_tok = P.dma("scalar", st_out[oc*128:(oc+1)*128, col:col+R], sc[:, :],
                           waits=last(tev))
            tok[("ost", oc % 2)] = st_tok
    P.dma("scalar", io["hout"][:, :], hcol[:, 0:16], waits=last(tok[("hc", 15)]))
    return st_tok



_BUILD_CACHE = {}

def kernel(**inputs):
    w = _stage_shared(inputs)
    key = "full"
    if key not in _BUILD_CACHE:
        _BUILD_CACHE[key] = build_program(PHASES_ALL)
    nc, io = _BUILD_CACHE[key]
    X = np.asarray(inputs["L0_attn_x"], np.float32)
    EPS = float(np.float32(2.0**-100))

    def mk_im(b, half):
        im = dict(w)
        xs = np.zeros((NT + 3, 1024), np.float32)
        xs[3:] = X[b, half*NT:(half+1)*NT]
        if half == 1:
            xs[0:3] = X[b, NT-3:NT]
        im["x"] = np.ascontiguousarray(xs.T)
        im["l0a_dist"] = _dist_matrix(half)
        im["cin"] = np.zeros((128, 33), np.float32)
        im["hin"] = np.zeros((128, 16), np.float32)
        im["l1a_lmask"] = np.full((128, 1), EPS if half == 0 else 1.0, np.float32)
        return im

    ims1 = [mk_im(b, 0) for b in range(4)]
    res1 = run_bass_kernel_spmd(nc, ims1, core_ids=[0, 1, 2, 3])
    ims2 = [mk_im(b, 1) for b in range(4)]
    for b in range(4):
        ims2[b]["cin"] = np.ascontiguousarray(res1.results[b]["cout"])
        ims2[b]["hin"] = np.ascontiguousarray(res1.results[b]["hout"])
    res2 = run_bass_kernel_spmd(nc, ims2, core_ids=[0, 1, 2, 3])
    out = np.empty((4, 8192, 1024), np.float32)
    for b in range(4):
        out[b, :NT] = res1.results[b]["out"].T
        out[b, NT:] = res2.results[b]["out"].T
    return out
